# version 19
# baseline (speedup 1.0000x reference)
import numpy as np
import concourse.bass as bass
import concourse.mybir as mybir
from concourse.bass_utils import run_bass_kernel_spmd

F32 = mybir.dt.float32
BF16 = mybir.dt.bfloat16
I8 = mybir.dt.int8
AF = mybir.ActivationFunctionType
ALU = mybir.AluOpType
AX = mybir.AxisListType

D = 1024
NB = D // 128
DH = 4096
NHB = DH // 128
SEG = 8
NCORE = 8
NGROUP = 4

PC_W, PC_U, PC_TMK, PC_TMV, PC_TMR, PC_S3 = 0, 8, 16, 24, 32, 40
PC_S1, PC_S2, PC_SH1, PC_SH2 = 48, 80, 112, 144
PC_TOT = 176


def build_nc(T=2048, use_collective=True, strict_sync=True, alias_sbuf=True,
             debug_dump=False):
    HALF = T // 2
    NSEGH = HALF // SEG
    NCHUNK = T // 128
    SW = T // 2
    PSB = max(SW, 512)
    NW = min(512, SW)

    nc = bass.Bass("TRN2", target_bir_lowering=False)

    xs = nc.dram_tensor("xs", [T + 1, D], F32, kind="ExternalInput").ap()
    pcol = nc.dram_tensor("pcol", [128, PC_TOT], F32, kind="ExternalInput").ap()
    cmask = nc.dram_tensor("cmask", [128, NGROUP], F32, kind="ExternalInput").ap()
    ident = nc.dram_tensor("ident", [128, 128], F32, kind="ExternalInput").ap()
    wk_d = nc.dram_tensor("wkT", [D, D], F32, kind="ExternalInput").ap()
    wv_d = nc.dram_tensor("wvT", [D, D], F32, kind="ExternalInput").ap()
    wr_d = nc.dram_tensor("wrT", [D, D], F32, kind="ExternalInput").ap()
    wo_d = nc.dram_tensor("woT", [D, D], F32, kind="ExternalInput").ap()
    w1_d = nc.dram_tensor("w1T", [D, DH], F32, kind="ExternalInput").ap()
    w2_d = nc.dram_tensor("w2T", [D, DH], F32, kind="ExternalInput").ap()
    h1_d = nc.dram_tensor("h1T", [DH, DH], F32, kind="ExternalInput").ap()
    h2_d = nc.dram_tensor("h2T", [DH, DH], F32, kind="ExternalInput").ap()
    w3_d = nc.dram_tensor("w3T", [DH, D], F32, kind="ExternalInput").ap()
    y = nc.dram_tensor("y", [T, D], F32, kind="ExternalOutput").ap()
    acc_hbm = nc.dram_tensor("acc_hbm", [NB, 128, T], F32, kind="Internal").ap()
    k_hbm = nc.dram_tensor("k_hbm", [NB, 128, T], F32, kind="Internal").ap()
    r_hbm = nc.dram_tensor("r_hbm", [NB, 128, T], BF16, kind="Internal").ap()
    b_hbm = nc.dram_tensor("b_hbm", [NHB // 2, 128, 2 * SW], BF16,
                           kind="Internal").ap()
    xn_hbm = nc.dram_tensor("xn_hbm", [128, NB * (T + 1)], BF16,
                            kind="Internal").ap()
    cc_in = nc.dram_tensor("cc_in", [128, 24], F32)
    cc_out = nc.dram_tensor("cc_out", [NGROUP * 128, 24], F32)
    if debug_dump:
        dbg_a = nc.dram_tensor("dbg_a", [128, NHB * SW], BF16,
                               kind="ExternalOutput").ap()
        dbg_wkv = nc.dram_tensor("dbg_wkv", [128, NB * T], F32,
                                 kind="ExternalOutput").ap()
        dbg_b = nc.dram_tensor("dbg_b", [128, NHB * SW], BF16,
                               kind="ExternalOutput").ap()
        dbg_c = nc.dram_tensor("dbg_c", [128, NHB * SW], BF16,
                               kind="ExternalOutput").ap()

    ENGS = ("sync", "gpsimd", "vector", "scalar", "tensor")
    streams = {e: [] for e in ENGS}
    SEMNAMES = ["sq", "sqx0", "sqx1", "gqw0", "gqw1", "gqf0", "gqf1",
                "gq0_0", "gq0_1", "gq0_2", "gq0_3",
                "gq1_0", "gq1_1", "gq1_2", "gq1_3",
                "sp0", "sp1", "spl", "spx", "sqc", "gqg", "sqk", "sqr", "sqr1", "sqb1", "cvh",
                "sqb", "sqa0", "sqa1", "sqf0", "sqf1", "sqxr",
                "out0", "out1", "cc", "T", "V", "S", "G"]
    cnt = {k: 0 for k in SEMNAMES}
    sems = {}
    STRICT_ENG = {"vector": "V", "scalar": "S"} if strict_sync else {}

    def op(eng, f, inc=None, n=1):
        if eng in STRICT_ENG:
            own = STRICT_ENG[eng]
            prev = cnt[own]
            if prev > 0:
                streams[eng].append(
                    lambda e, own=own, prev=prev: e.wait_ge(sems[own], prev))
            if inc is None:
                inc, n = own, 1
        if inc is None:
            streams[eng].append(lambda e, f=f: f(e))
            return None
        cnt[inc] += n
        if eng in STRICT_ENG and inc != STRICT_ENG[eng]:
            own = STRICT_ENG[eng]
            cnt[own] += 1
            streams[eng].append(
                lambda e, f=f, inc=inc, n=n, own=own:
                f(e).then_inc(sems[inc], n).then_inc(sems[own], 1))
        else:
            streams[eng].append(
                lambda e, f=f, inc=inc, n=n: f(e).then_inc(sems[inc], n))
        return cnt[inc]

    def wait(eng, sem, val):
        if val is None or val <= 0:
            return
        streams[eng].append(lambda e, sem=sem, val=val: e.wait_ge(sems[sem], val))

    guards = {}

    def alloc(name, shape, dtype, side="left"):
        g = nc.sbuf_tensor(name, shape, dtype, side=side)
        t = g.__enter__()
        guards[name] = g
        return t

    def free(*names):
        if not alias_sbuf:
            return
        for n in names:
            guards.pop(n).__exit__(None, None, None)

    v32 = alloc("v32", [128, NB * T], F32)
    abuf = alloc("abuf", [128, NHB * SW], BF16)
    wbuf = alloc("wbuf", [128, 2 * 1024], BF16)
    fp2 = alloc("fp2", [128, 2 * 1024], F32)
    prm = alloc("prm", [128, PC_TOT], F32)
    dpr = alloc("dpr", [128, 56], F32)
    smb = alloc("smb", [128, 24], F32)
    gat = alloc("gat", [128, NGROUP * 24], F32)
    carr = alloc("carr", [128, 48], F32)
    idt = alloc("idt", [128, 128], F32)
    io7 = alloc("io7", [128, HALF], I8)
    stt6 = alloc("stt6", [128, 16], F32)
    cmk = alloc("cmk", [128, NGROUP], F32)
    xnT = alloc("xnT", [128, NB * (T + 1)], BF16, side="right")

    d_expw = dpr[:, 0:8]
    d_expu = dpr[:, 8:16]
    d_w8 = dpr[:, 16:24]
    d_w7 = dpr[:, 24:32]
    d_wT = dpr[:, 32:40]
    d_eps = dpr[:, 40:41]
    d_wneg = dpr[:, 48:56]

    FPs = [fp2[:, 0:1024], fp2[:, 1024:2048]]
    FPs16 = [fp2[:, 0:1024].bitcast(BF16), fp2[:, 1024:2048].bitcast(BF16)]

    def xnT_b(b, t=None):
        ap = xnT if t is None else t
        return ap[:, b * (T + 1):(b + 1) * (T + 1)]

    def v_b(b):
        return v32[:, b * T:(b + 1) * T]

    psg = [nc.psum_tensor("pA", [128, 2048], F32),
           nc.psum_tensor("pB", [128, 2048], F32)]
    pA = psg[0].__enter__()
    pB = psg[1].__enter__()

    def seg3(ap2d):
        return ap2d.rearrange("p (s c) -> p s c", c=SEG)

    op("sync", lambda e: e.dma_start(out=prm[:, :], in_=pcol), inc="sq", n=16)
    op("sync", lambda e: e.dma_start(out=idt[:, :], in_=ident), inc="sq", n=16)
    op("sync", lambda e: e.dma_start(out=cmk[:, :], in_=cmask), inc="sq", n=16)
    ld_const = cnt["sq"]
    g_iota = op("gpsimd", lambda e: e.iota(
        io7[:, :], pattern=[[0, NSEGH], [1, SEG]], base=0, channel_multiplier=0,
        allow_small_or_imprecise_dtypes=True), inc="G")


    wait("scalar", "sq", ld_const)
    op("scalar", lambda e: e.activation(d_expw, prm[:, PC_W:PC_W + 8], AF.Exp))
    op("scalar", lambda e: e.activation(d_expu, prm[:, PC_U:PC_U + 8], AF.Exp))
    wait("vector", "sq", ld_const)
    op("vector", lambda e: e.memset(d_eps, 1e-5))
    op("vector", lambda e: e.tensor_scalar_mul(d_w8, prm[:, PC_W:PC_W + 8], 8.0))
    op("vector", lambda e: e.tensor_scalar_mul(d_w7, prm[:, PC_W:PC_W + 8], 7.0))
    op("vector", lambda e: e.tensor_scalar_mul(d_wT, prm[:, PC_W:PC_W + 8],
                                               float(T)))
    op("vector", lambda e: e.tensor_scalar_mul(d_wneg, prm[:, PC_W:PC_W + 8],
                                               -1.0), inc="V")

    xb = alloc("xb", [128, 2 * D], F32, side="right")
    nchunks_x = NCHUNK + 1
    tp_T = {}
    ev_S = {}
    for c in range(nchunks_x):
        rows = 128 if c < NCHUNK else 1
        slot = c % 2
        xsl = xb[:rows, slot * D:slot * D + D]
        if c >= 2:
            wait("sync", "T", tp_T[c - 2])
        sq_c = op("sync", lambda e, c=c, xsl=xsl, rows=rows:
                  e.dma_start(out=xsl, in_=xs[c * 128:c * 128 + rows, :]),
                  inc="sqx%d" % slot, n=16)
        wait("vector", "sqx%d" % slot, sq_c)
        op("vector", lambda e, xsl=xsl, rows=rows:
           e.bn_stats(stt6[:rows, 0:6], xsl[:, 0:512]))
        op("vector", lambda e, xsl=xsl, rows=rows:
           e.bn_stats(stt6[:rows, 6:12], xsl[:, 512:1024]))
        vst = op("vector", lambda e, rows=rows:
                 e.bn_aggr(stt6[:rows, 12:14], stt6[:rows, 0:12]), inc="V")
        wait("scalar", "V", vst)
        ssc = op("scalar", lambda e, rows=rows:
                 e.activation(stt6[:rows, 14:15], stt6[:rows, 13:14], AF.Ln,
                              bias=d_eps[:rows, :]), inc="S")
        wait("scalar", "S", ssc)
        ssc2 = op("scalar", lambda e, rows=rows:
                  e.activation(stt6[:rows, 15:16], stt6[:rows, 14:15], AF.Exp,
                               scale=-0.5), inc="S")
        wait("vector", "S", ssc2)
        vn = op("vector", lambda e, xsl=xsl, rows=rows:
                e.tensor_scalar(xsl, xsl, stt6[:rows, 12:13], stt6[:rows, 15:16],
                                op0=ALU.subtract, op1=ALU.mult), inc="V")
        pbank = pA if (c % 2 == 0) else pB
        wait("tensor", "V", vn)
        if c >= 2:
            wait("tensor", "S", ev_S[c - 2])
        tl = None
        for fb in range(NB):
            tl = op("tensor", lambda e, xsl=xsl, fb=fb, pbank=pbank, rows=rows:
                    e.transpose(pbank[:, fb * 128:fb * 128 + rows],
                                xsl[:, fb * 128:(fb + 1) * 128],
                                idt[:rows, :rows]), inc="T")
        tp_T[c] = tl
        wait("scalar", "T", tl)
        se = None
        for fb in range(NB):
            se = op("scalar", lambda e, c=c, fb=fb, pbank=pbank, rows=rows:
                    e.copy(xnT[:, fb * (T + 1) + c * 128:
                               fb * (T + 1) + c * 128 + rows],
                           pbank[:, fb * 128:fb * 128 + rows]), inc="S")
        ev_S[c] = se
    xnT_done_S = ev_S[nchunks_x - 1]
    free("xb")

    wait("sync", "S", xnT_done_S)
    spx_c = op("sync", lambda e: e.dma_start(out=xn_hbm, in_=xnT[:, :]),
               inc="spx", n=16)

    ws1 = w1_d.rearrange("(kb p) o -> p kb o", p=128)
    ws2 = w2_d.rearrange("(kb p) o -> p kb o", p=128)
    wsh1 = h1_d.rearrange("(kb p) o -> p kb o", p=128)
    wsh2 = h2_d.rearrange("(kb p) o -> p kb o", p=128)
    ws3 = w3_d.rearrange("(kb p) o -> p kb o", p=128)
    spill_state = {"idx": 0, "last_dma": {0: 0, 1: 0}}

    def emit_w12(st, base_xnT, psum_guard0, wbuf_guard_T, fwb_guard_T, a_dst,
                 w2buf):
        base = 1 + st * SW
        wmm_T = {}
        wev_V = {}
        for m in range(NHB):
            if m < 2 and wbuf_guard_T:
                wait("gpsimd", "T", wbuf_guard_T)
            if m < 2 and fwb_guard_T:
                wait("gpsimd", "T", fwb_guard_T)
            if m >= 2:
                wait("gpsimd", "T", wmm_T[m - 2])
            wd1 = op("gpsimd", lambda e, m=m:
                     e.dma_start(out=wbuf[:, (m % 2) * 1024:(m % 2) * 1024 + 1024]
                                 .rearrange("p (kb o) -> p kb o", o=128),
                                 in_=ws1[:, :, m * 128:(m + 1) * 128]),
                     inc="gqw%d" % (m % 2), n=16)
            w2slot = w2buf((m % 2))
            wd2 = op("gpsimd", lambda e, m=m, w2slot=w2slot:
                     e.dma_start(out=w2slot
                                 .rearrange("p (kb o) -> p kb o", o=128),
                                 in_=ws2[:, :, m * 128:(m + 1) * 128]),
                     inc="gqf%d" % (m % 2), n=16)
            wait("tensor", "gqw%d" % (m % 2), wd1)
            wait("tensor", "gqf%d" % (m % 2), wd2)
            if m == 0:
                for sem, val in psum_guard0:
                    wait("tensor", sem, val)
            pbank = pA if (m % 2 == 0) else pB
            if m >= 2:
                wait("tensor", "V", wev_V[m - 2])
            last = None
            for n in range(0, SW, NW):
                nwid = min(NW, SW - n)
                for kb in range(NB):
                    mov = base_xnT(kb)[:, base + n:base + n + nwid]
                    op("tensor",
                       lambda e, m=m, kb=kb, pbank=pbank, nwid=nwid, n=n, mov=mov:
                       e.matmul(pbank[:, n:n + nwid],
                                wbuf[:, (m % 2) * 1024 + kb * 128:
                                     (m % 2) * 1024 + kb * 128 + 128],
                                mov, start=(kb == 0), stop=(kb == NB - 1)),
                       inc="T")
            for n in range(0, SW, NW):
                nwid = min(NW, SW - n)
                for kb in range(NB):
                    mov = base_xnT(kb)[:, base + n:base + n + nwid]
                    last = op("tensor",
                              lambda e, m=m, kb=kb, pbank=pbank, nwid=nwid, n=n,
                              mov=mov, w2slot=w2slot:
                              e.matmul(pbank[:, PSB + n:PSB + n + nwid],
                                       w2slot[:, kb * 128:kb * 128 + 128],
                                       mov, start=(kb == 0), stop=(kb == NB - 1)),
                              inc="T")
            wmm_T[m] = last
            s1_m = prm[:, PC_S1 + m:PC_S1 + m + 1]
            s2_m = prm[:, PC_S2 + m:PC_S2 + m + 1]
            wait("scalar", "T", last)
            if m == 0:
                wait("scalar", "sp0", spill_state["last_dma"][0])
                wait("scalar", "sp1", spill_state["last_dma"][1])
            if m >= 1:
                wait("scalar", "V", wev_V[m - 1])
            ssg = op("scalar", lambda e, pbank=pbank, s1_m=s1_m:
                     e.activation(FPs[0][:, 0:SW], pbank[:, 0:SW], AF.Sigmoid,
                                  scale=s1_m), inc="S")
            wait("vector", "S", ssg)
            op("vector", lambda e, pbank=pbank, s2_m=s2_m:
               e.scalar_tensor_tensor(FPs[1][:, 0:SW], pbank[:, PSB:PSB + SW],
                                      s2_m, FPs[0][:, 0:SW],
                                      op0=ALU.mult, op1=ALU.mult))
            dap = a_dst(m)
            wev_V[m] = op("vector", lambda e, pbank=pbank, s1_m=s1_m, dap=dap:
                          e.scalar_tensor_tensor(dap, pbank[:, 0:SW], s1_m,
                                                 FPs[1][:, 0:SW],
                                                 op0=ALU.mult, op1=ALU.mult),
                          inc="V")
        return wmm_T[NHB - 1], wev_V[NHB - 1]

    w2b = alloc("w2b", [128, 2 * 1024], BF16, side="right")
    w12mm_T, a_done_V = emit_w12(0, lambda kb: xnT_b(kb),
                                 [("S", xnT_done_S)], None,
                                 tp_T[nchunks_x - 1],
                                 lambda m: abuf[:, m * SW:(m + 1) * SW],
                                 lambda s: w2b[:, s * 1024:(s + 1) * 1024])
    free("w2b")

    xmix = alloc("xmix", [128, NB * T], BF16, side="right")

    def emit_lerp(pc_off, pre_waits=()):
        for sem, val in pre_waits:
            wait("vector", sem, val)
        last = None
        for b in range(NB):
            tm = prm[:, pc_off + b:pc_off + b + 1]
            for h in range(2):
                cur = xnT_b(b)[:, h * HALF + 1:h * HALF + 1 + HALF]
                prv = xnT_b(b)[:, h * HALF:h * HALF + HALF]
                dst = xmix[:, b * T + h * HALF:b * T + h * HALF + HALF]
                op("vector", lambda e, cur=cur, prv=prv, dst=dst:
                   e.tensor_tensor(dst, cur, prv, op=ALU.subtract))
                last = op("vector", lambda e, dst=dst, prv=prv, tm=tm:
                          e.scalar_tensor_tensor(dst, dst, tm, prv,
                                                 op0=ALU.mult, op1=ALU.add),
                          inc="V")
        return last

    def emit_proj(w_dram, src, lerp_V, psum_guards, evac, wbuf_guard=None,
                  interleave=None, psum_guards1=()):
        wsrc = w_dram.rearrange("(kb p) o -> p kb o", p=128)
        mm_T = {}
        evc = {}
        ev_sem = None
        for m in range(NB):
            if m < 2 and wbuf_guard:
                wait("gpsimd", "T", wbuf_guard)
            if m >= 2:
                wait("gpsimd", "T", mm_T[m - 2])
            wdma = op("gpsimd", lambda e, m=m, wsrc=wsrc:
                      e.dma_start(out=wbuf[:, (m % 2) * 1024:(m % 2) * 1024 + 1024]
                                  .rearrange("p (kb o) -> p kb o", o=128),
                                  in_=wsrc[:, :, m * 128:(m + 1) * 128]),
                      inc="gqw%d" % (m % 2), n=16)
            wait("tensor", "gqw%d" % (m % 2), wdma)
            if m == 0:
                wait("tensor", "V", lerp_V)
                for sem, val in psum_guards:
                    wait("tensor", sem, val)
            if m == 1:
                for sem, val in psum_guards1:
                    wait("tensor", sem, val)
            pbank = pA if (m % 2 == 0) else pB
            if m >= 2:
                wait("tensor", ev_sem, evc[m - 2])
            last = None
            for n in range(0, T, 512):
                nwid = min(512, T - n)
                for kb in range(NB):
                    mov = src(kb)[:, n:n + nwid]
                    last = op("tensor",
                              lambda e, m=m, kb=kb, pbank=pbank, nwid=nwid,
                              n=n, mov=mov:
                              e.matmul(pbank[:, n:n + nwid],
                                       wbuf[:, (m % 2) * 1024 + kb * 128:
                                            (m % 2) * 1024 + kb * 128 + 128],
                                       mov,
                                       start=(kb == 0), stop=(kb == NB - 1)),
                              inc="T")
            mm_T[m] = last
            ev_sem, evc[m] = evac(m, pbank, last)
            if interleave is not None:
                interleave(m)
        return mm_T[NB - 1], evc[NB - 1]

    def spill_evac_f32(dst_hbm):
        def evac(m, pbank, mm_last):
            wait("scalar", "T", mm_last)
            last = None
            for off in range(0, T, 1024):
                w = min(1024, T - off)
                sidx = spill_state["idx"] % 2
                spill_state["idx"] += 1
                prev_dma = spill_state["last_dma"][sidx]
                if prev_dma:
                    wait("scalar", "sp%d" % sidx, prev_dma)
                for n2 in range(0, w, 512):
                    n2w = min(512, w - n2)
                    last = op("scalar",
                              lambda e, off=off, n2=n2, pbank=pbank, sidx=sidx,
                              n2w=n2w:
                              e.copy(FPs[sidx][:, n2:n2 + n2w],
                                     pbank[:, off + n2:off + n2 + n2w]),
                              inc="S")
                wait("sync", "S", last)
                dmac = op("sync", lambda e, m=m, off=off, sidx=sidx, w=w,
                          dst_hbm=dst_hbm:
                          e.dma_start(out=dst_hbm[m][:, off:off + w],
                                      in_=FPs[sidx][:, 0:w]),
                          inc="sp%d" % sidx, n=16)
                spill_state["last_dma"][sidx] = dmac
            return "S", last
        return evac

    def spill_evac_sigmoid16(dst_hbm):
        def evac(m, pbank, mm_last):
            wait("scalar", "T", mm_last)
            sidx = spill_state["idx"] % 2
            spill_state["idx"] += 1
            prev_dma = spill_state["last_dma"][sidx]
            if prev_dma:
                wait("scalar", "sp%d" % sidx, prev_dma)
            last = None
            for n in range(0, T, 512):
                nwid = min(512, T - n)
                last = op("scalar", lambda e, n=n, pbank=pbank, sidx=sidx,
                          nwid=nwid:
                          e.activation(FPs16[sidx][:, n:n + nwid],
                                       pbank[:, n:n + nwid], AF.Sigmoid),
                          inc="S")
            wait("sync", "S", last)
            dmac = op("sync", lambda e, m=m, sidx=sidx, dst_hbm=dst_hbm:
                      e.dma_start(out=dst_hbm[m], in_=FPs16[sidx][:, 0:T]),
                      inc="sp%d" % sidx, n=16)
            spill_state["last_dma"][sidx] = dmac
            return "S", last
        return evac

    def copy_evac(dst, rec=None):
        def evac(m, pbank, mm_last):
            wait("scalar", "T", mm_last)
            last = None
            for n in range(0, T, 512):
                nwid = min(512, T - n)
                dap = dst(m)[:, n:n + nwid]
                last = op("scalar", lambda e, pbank=pbank, nwid=nwid, n=n, dap=dap:
                          e.copy(dap, pbank[:, n:n + nwid]), inc="S")
            if rec is not None:
                rec[m] = last
            return "S", last
        return evac

    def xmix_src(kb):
        return xmix[:, kb * T:(kb + 1) * T]

    lerpr_V = emit_lerp(PC_TMR, pre_waits=[("S", xnT_done_S)])
    wait("scalar", "V", a_done_V)
    rmm_T, rev_S = emit_proj(wr_d, xmix_src, lerpr_V,
                             [("S", xnT_done_S), ("V", a_done_V)],
                             spill_evac_sigmoid16(r_hbm), wbuf_guard=w12mm_T)
    lerpk_V = emit_lerp(PC_TMK, pre_waits=[("T", rmm_T)])
    kmm_T, kev_S = emit_proj(wk_d, xmix_src, lerpk_V,
                             [("S", rev_S)], spill_evac_f32(k_hbm),
                             wbuf_guard=rmm_T)
    lerpv_V = emit_lerp(PC_TMV, pre_waits=[
        ("T", kmm_T), ("sp0", spill_state["last_dma"][0]),
        ("sp1", spill_state["last_dma"][1])])
    k_spills_done = dict(spill_state["last_dma"])

    free("xmix")
    free("xnT")
    PW = HALF + 8
    pool_a = alloc("pool_a", [128, 4 * PW], F32, side="right")
    kstr = alloc("kstr", [128, HALF], F32, side="right")
    CW = NSEGH + 1
    cpa = alloc("cpa", [128, 12 * CW], F32, side="right")
    hst = alloc("hst", [128, 8], F32, side="right")
    rbb = alloc("rbb", [128, T], BF16, side="right")
    C = [cpa[:, i * CW:(i + 1) * CW] for i in range(12)]
    GB = C[11]

    kload_cnt = {"n": 0}
    kstr_guard_V = [0]

    def emit_kload(b, h):
        wait("sync", "V", kstr_guard_V[0])
        if kload_cnt["n"] == 0:
            wait("sync", "spx", spx_c)
            wait("sync", "V", lerpv_V)
        kload_cnt["n"] += 1
        return op("sync", lambda e, b=b, h=h:
                  e.dma_start(out=kstr[:, :],
                              in_=k_hbm[b][:, h * HALF:(h + 1) * HALF]),
                  inc="sqk", n=16)

    S = [pool_a[:, i * PW:(i + 1) * PW] for i in range(4)]
    v32b16 = v32[:, :].bitcast(BF16)

    def rw16_b(b):
        return v32b16[:, b * 2 * T:b * 2 * T + T]

    RB = [rbb[:, 0:HALF], rbb[:, HALF:2 * HALF]]
    v_ev = {}
    p1_state = {"V": None}

    def emit_p1_block(b):
        if b == 0:
            wait("vector", "G", g_iota)
            wait("vector", "spx", spx_c)
            wait("vector", "V", lerpv_V)
        w_b = prm[:, PC_W + b:PC_W + b + 1]
        w8_b = d_w8[:, b:b + 1]
        w7_b = d_w7[:, b:b + 1]
        for h in range(2):
            kl = emit_kload(b, h)
            vs = v_b(b)[:, h * HALF:(h + 1) * HALF]
            wn_b = d_wneg[:, b:b + 1]
            wait("vector", "sqk", kl)
            kdone = op("vector", lambda e, wn_b=wn_b: e.scalar_tensor_tensor(
                S[1][:, 0:HALF], io7[:, :], wn_b, kstr[:, :],
                op0=ALU.mult, op1=ALU.add), inc="V")
            kstr_guard_V[0] = kdone
            op("vector", lambda e: e.tensor_reduce(
                C[0][:, 0:NSEGH], seg3(S[1][:, 0:HALF]), axis=AX.X, op=ALU.max))
            vv = op("vector", lambda e: e.tensor_tensor(
                seg3(S[2][:, 0:HALF]), seg3(S[1][:, 0:HALF]),
                C[0][:, 0:NSEGH].unsqueeze(2).broadcast_to([128, NSEGH, SEG]),
                op=ALU.subtract), inc="V")
            wait("scalar", "V", vv)
            sv = op("scalar", lambda e: e.activation(
                S[2][:, 0:HALF], S[2][:, 0:HALF], AF.Exp), inc="S")
            op("vector", lambda e, w7_b=w7_b: e.tensor_scalar_add(
                C[3][:, 0:NSEGH], C[0][:, 0:NSEGH], w7_b))
            if h == 0:
                op("vector", lambda e: e.memset(C[4][:, 0:1], 0.0))
            else:
                op("vector", lambda e: e.tensor_copy(C[4][:, 0:1], hst[:, 2:3]))
            op("vector", lambda e, w8_b=w8_b: e.tensor_tensor_scan(
                C[4][:, 1:1 + NSEGH], w8_b.broadcast_to([128, NSEGH]),
                C[3][:, 0:NSEGH], initial=C[4][:, 0:1],
                op0=ALU.add, op1=ALU.max))
            op("vector", lambda e: e.tensor_tensor(
                C[5][:, 0:NSEGH], C[4][:, 0:NSEGH], C[4][:, 1:1 + NSEGH],
                op=ALU.subtract))
            vv = op("vector", lambda e: e.tensor_tensor(
                C[6][:, 0:NSEGH], C[3][:, 0:NSEGH], C[4][:, 1:1 + NSEGH],
                op=ALU.subtract), inc="V")
            wait("scalar", "S", sv)
            wait("scalar", "V", vv)
            op("scalar", lambda e, w8_b=w8_b: e.activation(
                C[5][:, 0:NSEGH], C[5][:, 0:NSEGH], AF.Exp, bias=w8_b))
            sv2 = op("scalar", lambda e: e.activation(
                C[6][:, 0:NSEGH], C[6][:, 0:NSEGH], AF.Exp), inc="S")
            wait("vector", "S", sv)
            wait("vector", "S", v_ev[b])
            op("vector", lambda e, vs=vs: e.tensor_tensor(
                S[3][:, 0:HALF], S[2][:, 0:HALF], vs, op=ALU.mult))
            op("vector", lambda e: e.tensor_reduce(
                C[1][:, 0:NSEGH], seg3(S[3][:, 0:HALF]), axis=AX.X, op=ALU.add))
            op("vector", lambda e: e.tensor_reduce(
                C[2][:, 0:NSEGH], seg3(S[2][:, 0:HALF]), axis=AX.X, op=ALU.add))
            wait("vector", "S", sv2)
            op("vector", lambda e: e.tensor_mul(
                C[7][:, 0:NSEGH], C[6][:, 0:NSEGH], C[1][:, 0:NSEGH]))
            op("vector", lambda e: e.tensor_mul(
                C[8][:, 0:NSEGH], C[6][:, 0:NSEGH], C[2][:, 0:NSEGH]))
            if h == 0:
                op("vector", lambda e: e.memset(C[9][:, 0:1], 0.0))
                op("vector", lambda e: e.memset(C[10][:, 0:1], 0.0))
            else:
                op("vector", lambda e: e.tensor_copy(C[9][:, 0:1], hst[:, 0:1]))
                op("vector", lambda e: e.tensor_copy(C[10][:, 0:1], hst[:, 1:2]))
            op("vector", lambda e: e.tensor_tensor_scan(
                C[9][:, 1:1 + NSEGH], C[5][:, 0:NSEGH], C[7][:, 0:NSEGH],
                initial=C[9][:, 0:1], op0=ALU.mult, op1=ALU.add))
            op("vector", lambda e: e.tensor_tensor_scan(
                C[10][:, 1:1 + NSEGH], C[5][:, 0:NSEGH], C[8][:, 0:NSEGH],
                initial=C[10][:, 0:1], op0=ALU.mult, op1=ALU.add))
            op("vector", lambda e: e.tensor_copy(hst[:, 0:1],
                                                 C[9][:, NSEGH:NSEGH + 1]))
            op("vector", lambda e: e.tensor_copy(hst[:, 1:2],
                                                 C[10][:, NSEGH:NSEGH + 1]))
            op("vector", lambda e: e.tensor_copy(hst[:, 2:3],
                                                 C[4][:, NSEGH:NSEGH + 1]))
        op("vector", lambda e, b=b: e.tensor_copy(smb[:, b:b + 1], hst[:, 0:1]))
        op("vector", lambda e, b=b: e.tensor_copy(smb[:, 8 + b:9 + b], hst[:, 1:2]))
        p1_state["V"] = op("vector", lambda e, b=b: e.tensor_copy(
            smb[:, 16 + b:17 + b], hst[:, 2:3]), inc="V")

    P1_LAG = 3
    vmm_T, vev_S = emit_proj(wv_d, xmix_src, lerpv_V,
                             [("S", kev_S)], copy_evac(v_b, rec=v_ev),
                             wbuf_guard=kmm_T,
                             interleave=lambda m: (emit_p1_block(m - P1_LAG)
                                                   if m >= P1_LAG else None))

    WD_ENG = {0: "gpsimd", 1: "gpsimd"}

    def emit_h_wd_q(wsrc, mp, hf, q):
        return [op(WD_ENG[hf], lambda e, mp=mp, hf=hf, q=q, wsrc=wsrc:
                e.dma_start(
                    out=fwb[:, hf * 4096 + q * 2048:hf * 4096 + (q + 1) * 2048]
                    .rearrange("p (kb o) -> p kb o", o=256),
                    in_=wsrc[:, hf * 16 + q * 8:hf * 16 + (q + 1) * 8,
                             mp * 256:(mp + 1) * 256]),
                inc="gq%d_%d" % (hf, q), n=16)]

    def emit_h_wd(wsrc, mp, hf):
        return emit_h_wd_q(wsrc, mp, hf, 0) + emit_h_wd_q(wsrc, mp, hf, 1)

    def emit_h(wsrc, src16, dst_evac, first_guards, fwb_guard, xn_guard=False,
               interleave=None, preload=None, inject=None):
        hmm_T = {}
        hmm_half = {}
        hmm_q = {}
        hev = {}
        for mp in range(NHB // 2):
            if mp == 0 and preload is not None:
                wds = preload
            else:
                wds = {}
                for hf in range(2):
                    eng = WD_ENG[hf]
                    if mp == 0 and fwb_guard[0]:
                        wait(eng, "T", fwb_guard[0])
                    if mp == 0 and xn_guard:
                        wait(eng, "spx", spx_c)
                        wait(eng, "T", vmm_T)
                        wait(eng, "V", lerpv_V)
                    if mp >= 1:
                        wds[hf] = []
                        for q in range(2):
                            wait(eng, "T", hmm_q[(mp - 1, hf, q)])
                            wds[hf] += emit_h_wd_q(wsrc, mp, hf, q)
                    else:
                        wds[hf] = emit_h_wd(wsrc, mp, hf)
            if inject is not None and mp in inject:
                inject[mp]()
            pbank = pA if (mp % 2 == 0) else pB
            if mp == 0:
                for sem, val in first_guards:
                    wait("tensor", sem, val)
            if mp >= 2:
                sem, val = hev[mp - 2]
                wait("tensor", sem, val)
            last = None
            for hf in range(2):
                for kb in range(16):
                    if kb % 8 == 0:
                        wait("tensor", "gq%d_%d" % (hf, kb // 8),
                             wds[hf][kb // 8])
                    kbg = hf * 16 + kb
                    for mi in range(2):
                        for n in range(0, SW, NW):
                            nwid = min(NW, SW - n)
                            mov = src16(kbg)[:, n:n + nwid]
                            last = op(
                                "tensor",
                                lambda e, hf=hf, kb=kb, mi=mi, n=n, nwid=nwid,
                                pbank=pbank, mov=mov, kbg=kbg:
                                e.matmul(
                                    pbank[:, mi * PSB + n:mi * PSB + n + nwid],
                                    fwb[:, hf * 4096 + kb * 256 + mi * 128:
                                         hf * 4096 + kb * 256 + mi * 128 + 128],
                                    mov, start=(kbg == 0), stop=(kbg == 31)),
                                inc="T")
                    if kb % 8 == 7:
                        hmm_q[(mp, hf, kb // 8)] = last
                hmm_half[(mp, hf)] = last
            hmm_T[mp] = last
            hev[mp] = dst_evac(mp, pbank, last)
            if interleave is not None:
                interleave(mp)
        fwb_guard[0] = hmm_T[NHB // 2 - 1]
        return hev

    wait("gpsimd", "spx", spx_c)
    wait("gpsimd", "T", vmm_T)
    wait("gpsimd", "V", lerpv_V)
    h1_pre = {0: emit_h_wd(wsh1, 0, 0), 1: emit_h_wd(wsh1, 0, 1)}

    exch = {"cfin": None, "carry_V": None, "sq1": None}
    if use_collective:
        def emit_sq1():
            wait("sync", "V", p1_state["V"])
            exch["sq1"] = op("sync", lambda e:
                             e.dma_start(out=cc_in.ap(), in_=smb[:, :]),
                             inc="sqc", n=16)

        def emit_exchange():
            wait("gpsimd", "sqc", exch["sq1"])
            ccv = op("gpsimd", lambda e: e.collective_compute(
                "AllGather", ALU.bypass,
                replica_groups=[[0, 1, 2, 3], [4, 5, 6, 7]],
                ins=[cc_in.ap().opt()], outs=[cc_out.ap().opt()]), inc="cc", n=1)
            wait("gpsimd", "cc", ccv)
            gq1 = op("gpsimd", lambda e: e.dma_start(
                out=gat[:, :].rearrange("p (r c) -> p r c", c=24),
                in_=cc_out.ap().rearrange("(r p) c -> p r c", p=128)),
                inc="gqg", n=16)
            wait("vector", "gqg", gq1)
            cA, cB = carr[:, 0:24], carr[:, 24:48]
            op("vector", lambda e: e.memset(cA, 0.0))
            carry_V = None
            for r in range(NGROUP - 1):
                src_, dst = (cA, cB) if r % 2 == 0 else (cB, cA)
                m_r = cmk[:, r:r + 1]
                ga = gat[:, r * 24:r * 24 + 8]
                gb = gat[:, r * 24 + 8:r * 24 + 16]
                gq_ = gat[:, r * 24 + 16:r * 24 + 24]
                t_a, t_b, t_q, t_d, u1 = (C[0][:, 0:8], C[1][:, 0:8], C[2][:, 0:8],
                                          C[3][:, 0:8], C[4][:, 0:8])
                e1, e2 = C[5][:, 0:8], C[6][:, 0:8]
                op("vector", lambda e, ga=ga, m_r=m_r, t_a=t_a:
                   e.tensor_scalar_mul(t_a, ga, m_r))
                op("vector", lambda e, gb=gb, m_r=m_r, t_b=t_b:
                   e.tensor_scalar_mul(t_b, gb, m_r))
                op("vector", lambda e, gq_=gq_, m_r=m_r, t_q=t_q:
                   e.tensor_scalar_mul(t_q, gq_, m_r))
                op("vector", lambda e, m_r=m_r, t_d=t_d:
                   e.tensor_scalar_mul(t_d, d_wT, m_r))
                op("vector", lambda e, src_=src_, t_d=t_d, u1=u1:
                   e.tensor_add(u1, src_[:, 16:24], t_d))
                op("vector", lambda e, dst=dst, u1=u1, t_q=t_q:
                   e.tensor_max(dst[:, 16:24], u1, t_q))
                op("vector", lambda e, dst=dst, u1=u1, e1=e1:
                   e.tensor_tensor(e1, u1, dst[:, 16:24], op=ALU.subtract))
                vv = op("vector", lambda e, dst=dst, t_q=t_q, e2=e2:
                        e.tensor_tensor(e2, t_q, dst[:, 16:24], op=ALU.subtract),
                        inc="V")
                wait("scalar", "V", vv)
                op("scalar", lambda e, e1=e1: e.activation(e1, e1, AF.Exp))
                sv = op("scalar", lambda e, e2=e2: e.activation(e2, e2, AF.Exp),
                        inc="S")
                wait("vector", "S", sv)
                op("vector", lambda e, src_=src_, e1=e1, u1=u1:
                   e.tensor_mul(u1, src_[:, 0:8], e1))
                op("vector", lambda e, dst=dst, t_a=t_a, e2=e2:
                   e.tensor_mul(dst[:, 0:8], t_a, e2))
                op("vector", lambda e, dst=dst, u1=u1:
                   e.tensor_add(dst[:, 0:8], dst[:, 0:8], u1))
                op("vector", lambda e, src_=src_, e1=e1, u1=u1:
                   e.tensor_mul(u1, src_[:, 8:16], e1))
                op("vector", lambda e, dst=dst, t_b=t_b, e2=e2:
                   e.tensor_mul(dst[:, 8:16], t_b, e2))
                carry_V = op("vector", lambda e, dst=dst, u1=u1:
                             e.tensor_add(dst[:, 8:16], dst[:, 8:16], u1), inc="V")
            exch["cfin"] = cA if (NGROUP - 1) % 2 == 0 else cB
            exch["carry_V"] = carry_V
    else:
        def emit_sq1():
            pass

        def emit_exchange():
            cfin = carr[:, 0:24]
            exch["cfin"] = cfin
            exch["carry_V"] = op("vector", lambda e: e.memset(cfin, 0.0),
                                 inc="V")

    fwb = alloc("fwb", [128, 2 * 4096], BF16)
    pool_b = alloc("pool_b", [128, 3 * PW], F32)
    S = S + [pool_b[:, i * PW:(i + 1) * PW] for i in range(3)]

    scan_state = {"done_V": None}
    ruse = {}

    def emit_scan_chunk(i):
        b, h = i // 2, i % 2
        w_b = prm[:, PC_W + b:PC_W + b + 1]
        w8_b = d_w8[:, b:b + 1]
        expw_b = d_expw[:, b:b + 1]
        expu_b = d_expu[:, b:b + 1]
        a_in = exch["cfin"][:, b:b + 1]
        b_in = exch["cfin"][:, 8 + b:9 + b]
        q_in = exch["cfin"][:, 16 + b:17 + b]
        kl = emit_kload(b, h)
        j = 2 * b + h
        if j < 2:
            wait("sync", "sp0", k_spills_done[0])
            wait("sync", "sp1", k_spills_done[1])
        else:
            wait("sync", "V", ruse[j - 2])
        rl = op("sync", lambda e, b=b, h=h, j=j:
                e.dma_start(out=RB[j % 2][:, :],
                            in_=r_hbm[b][:, h * HALF:(h + 1) * HALF]),
                inc="sqr" if j % 2 == 0 else "sqr1", n=16)
        vs = v_b(b)[:, h * HALF:(h + 1) * HALF]
        if b == 0 and h == 0:
            wait("vector", "V", exch["carry_V"])
            wait("vector", "T", vmm_T)
        wait("vector", "sqk", kl)
        op("vector", lambda e: e.tensor_reduce(
            C[0][:, 0:NSEGH], seg3(kstr[:, :]), axis=AX.X, op=ALU.max))
        if h == 0:
            op("vector", lambda e, q_in=q_in:
               e.tensor_copy(GB[:, 0:1], q_in))
        else:
            op("vector", lambda e: e.tensor_copy(GB[:, 0:1], hst[:, 2:3]))
        op("vector", lambda e, w8_b=w8_b: e.tensor_tensor_scan(
            GB[:, 1:1 + NSEGH], w8_b.broadcast_to([128, NSEGH]),
            C[0][:, 0:NSEGH], initial=GB[:, 0:1], op0=ALU.add, op1=ALU.max))
        vv = op("vector", lambda e: e.tensor_tensor(
            C[1][:, 0:NSEGH], GB[:, 0:NSEGH], GB[:, 1:1 + NSEGH],
            op=ALU.subtract), inc="V")
        wait("scalar", "V", vv)
        op("scalar", lambda e, w_b=w_b: e.activation(
            C[2][:, 0:NSEGH], C[1][:, 0:NSEGH], AF.Exp, bias=w_b))
        sv = op("scalar", lambda e: e.activation(
            C[3][:, 0:NSEGH], C[1][:, 0:NSEGH], AF.Exp, scale=-1.0),
            inc="S")
        kdone = op("vector", lambda e: e.tensor_tensor(
            seg3(S[1][:, 0:HALF]), seg3(kstr[:, :]),
            GB[:, 1:1 + NSEGH].unsqueeze(2).broadcast_to([128, NSEGH, SEG]),
            op=ALU.subtract), inc="V")
        kstr_guard_V[0] = kdone
        op("vector", lambda e, expw_b=expw_b: e.tensor_scalar(
            S[0][:, 0:HALF], S[1][:, 0:HALF], 0.0, expw_b,
            op0=ALU.mult, op1=ALU.add))
        wait("vector", "S", sv)
        vv = op("vector", lambda e: e.tensor_copy(
            seg3(S[0][:, 0:HALF])[:, :, 0:1],
            C[2][:, 0:NSEGH].unsqueeze(2)), inc="V")
        wait("scalar", "V", vv)
        sv = op("scalar", lambda e: e.activation(
            S[1][:, 0:HALF], S[1][:, 0:HALF], AF.Exp), inc="S")
        wait("vector", "S", sv)
        op("vector", lambda e, expu_b=expu_b: e.tensor_scalar_mul(
            S[2][:, 0:HALF], S[1][:, 0:HALF], expu_b))
        op("vector", lambda e: e.tensor_tensor(
            seg3(S[2][:, 0:HALF])[:, :, 0:1], seg3(S[2][:, 0:HALF])[:, :, 0:1],
            C[3][:, 0:NSEGH].unsqueeze(2), op=ALU.mult))
        op("vector", lambda e, vs=vs: e.tensor_tensor(
            S[3][:, 0:HALF], S[2][:, 0:HALF], vs, op=ALU.mult))
        op("vector", lambda e, vs=vs: e.tensor_tensor(
            S[4][:, 0:HALF], S[1][:, 0:HALF], vs, op=ALU.mult))
        if h == 0:
            op("vector", lambda e, a_in=a_in: e.tensor_copy(S[5][:, 0:1], a_in))
            op("vector", lambda e, b_in=b_in: e.tensor_copy(S[6][:, 0:1], b_in))
        else:
            op("vector", lambda e: e.tensor_copy(S[5][:, 0:1], hst[:, 0:1]))
            op("vector", lambda e: e.tensor_copy(S[6][:, 0:1], hst[:, 1:2]))
        op("vector", lambda e: e.tensor_tensor_scan(
            S[5][:, 1:1 + HALF], S[0][:, 0:HALF], S[4][:, 0:HALF],
            initial=S[5][:, 0:1], op0=ALU.mult, op1=ALU.add))
        op("vector", lambda e: e.tensor_tensor_scan(
            S[6][:, 1:1 + HALF], S[0][:, 0:HALF], S[1][:, 0:HALF],
            initial=S[6][:, 0:1], op0=ALU.mult, op1=ALU.add))
        op("vector", lambda e: e.tensor_copy(hst[:, 0:1],
                                             S[5][:, HALF:HALF + 1]))
        op("vector", lambda e: e.tensor_copy(hst[:, 1:2],
                                             S[6][:, HALF:HALF + 1]))
        op("vector", lambda e: e.tensor_copy(hst[:, 2:3],
                                             GB[:, NSEGH:NSEGH + 1]))
        op("vector", lambda e: e.tensor_tensor(
            S[0][:, 0:HALF], S[5][:, 0:HALF], S[3][:, 0:HALF], op=ALU.add))
        op("vector", lambda e: e.tensor_tensor(
            S[3][:, 0:HALF], S[6][:, 0:HALF], S[2][:, 0:HALF], op=ALU.add))
        op("vector", lambda e: e.reciprocal(
            S[1][:, 0:HALF], S[3][:, 0:HALF]))
        op("vector", lambda e: e.tensor_tensor(
            S[0][:, 0:HALF], S[0][:, 0:HALF], S[1][:, 0:HALF], op=ALU.mult))
        wait("vector", "sqr" if j % 2 == 0 else "sqr1", rl)
        rwh = rw16_b(b)[:, h * HALF:(h + 1) * HALF]
        scan_state["done_V"] = op("vector", lambda e, rwh=rwh, j=j:
                                  e.tensor_tensor(rwh, S[0][:, 0:HALF],
                                                  RB[j % 2][:, :],
                                                  op=ALU.mult), inc="V")
        ruse[j] = scan_state["done_V"]


    def gelu_spill_evac(sc_off):
        def evac(mp, pbank, mm_last):
            wait("scalar", "T", mm_last)
            sidx = spill_state["idx"] % 2
            spill_state["idx"] += 1
            prev_dma = spill_state["last_dma"][sidx]
            if prev_dma:
                wait("scalar", "sp%d" % sidx, prev_dma)
            if mp == 0:
                wait("scalar", "V", a_done_V)
            e_last = None
            for mi in range(2):
                m = mp * 2 + mi
                sc = prm[:, sc_off + m:sc_off + m + 1]
                e_last = op("scalar",
                            lambda e, mi=mi, sc=sc, pbank=pbank, sidx=sidx:
                            e.activation(FPs16[sidx][:, mi * SW:(mi + 1) * SW],
                                         pbank[:, mi * PSB:mi * PSB + SW],
                                         AF.Gelu, scale=sc), inc="S")
            wait("sync", "S", e_last)
            dmac = op("sync", lambda e, mp=mp, sidx=sidx:
                      e.dma_start(out=b_hbm[mp], in_=FPs16[sidx][:, 0:2 * SW]),
                      inc="sp%d" % sidx, n=16)
            spill_state["last_dma"][sidx] = dmac
            return "S", e_last
        return evac

    fwb_guard = [None]
    SLOT = {mp: [mp - 4] for mp in range(4, NHB // 2)}

    def scan_slots(mp):
        if mp <= 2:
            emit_p1_block(NB - P1_LAG + mp)
        if mp == 3:
            emit_sq1()
        for j in SLOT.get(mp, []):
            emit_scan_chunk(j)

    h1_done = emit_h(wsh1, lambda kb: abuf[:, kb * SW:(kb + 1) * SW],
                     gelu_spill_evac(PC_SH1),
                     [("V", a_done_V), ("S", vev_S)], fwb_guard,
                     xn_guard=True, interleave=scan_slots,
                     preload=h1_pre, inject={4: emit_exchange})
    for j in range(NHB // 2 - 4, 2 * NB):
        emit_scan_chunk(j)
    b_spills_done = dict(spill_state["last_dma"])

    scan_done_V = scan_state["done_V"]

    rw_V = scan_state["done_V"]
    b16 = v32[:, :].bitcast(BF16)
    bload = {}
    for mp in range(1, NHB // 2, 2):
        if mp == 1:
            wait("gpsimd", "V", scan_state["done_V"])
            wait("gpsimd", "sp0", b_spills_done[0])
            wait("gpsimd", "sp1", b_spills_done[1])
        bload[mp] = op("gpsimd", lambda e, mp=mp:
                       e.dma_start(out=b16[:, mp * 2 * SW:(mp + 1) * 2 * SW],
                                   in_=b_hbm[mp]), inc="sqb1", n=16)
    omm_T, oev_S = emit_proj(wo_d, rw16_b, rw_V,
                             [h1_done[NHB // 2 - 2]],
                             spill_evac_f32(acc_hbm), wbuf_guard=w12mm_T,
                             psum_guards1=[h1_done[NHB // 2 - 1]])
    acc_sp_done = dict(spill_state["last_dma"])

    free("rbb", "hst", "cpa", "kstr", "pool_a")
    finpool = alloc("finpool", [128, 4 * 1024], F32, side="right")
    FN = [finpool[:, i * 1024:(i + 1) * 1024] for i in range(4)]

    for mp in range(0, NHB // 2, 2):
        if mp == 0:
            wait("sync", "T", omm_T)
            wait("sync", "sp0", b_spills_done[0])
            wait("sync", "sp1", b_spills_done[1])
        bload[mp] = op("sync", lambda e, mp=mp:
                       e.dma_start(out=b16[:, mp * 2 * SW:(mp + 1) * 2 * SW],
                                   in_=b_hbm[mp]), inc="sqb", n=16)

    def gelu_evac(dst16, sc_off):
        def evac(mp, pbank, mm_last):
            wait("scalar", "T", mm_last)
            e_last = None
            for mi in range(2):
                m = mp * 2 + mi
                sc = prm[:, sc_off + m:sc_off + m + 1]
                dap = dst16(m)
                e_last = op("scalar",
                            lambda e, mi=mi, sc=sc, pbank=pbank, dap=dap:
                            e.activation(dap, pbank[:, mi * PSB:mi * PSB + SW],
                                         AF.Gelu, scale=sc), inc="S")
            return "S", e_last
        return evac

    if debug_dump:
        wait("sync", "sqb", bload[NHB // 2 - 1])
        op("sync", lambda e: e.dma_start(out=dbg_b, in_=b16[:, :]),
           inc="sq", n=16)
        wait("sync", "sq", cnt["sq"])
    wait("tensor", "sqb", bload[NHB // 2 - 2])
    wait("tensor", "sqb1", bload[NHB // 2 - 1])
    h2_done = emit_h(wsh2, lambda kb: b16[:, kb * SW:(kb + 1) * SW],
                     gelu_evac(lambda m: abuf[:, m * SW:(m + 1) * SW], PC_SH2),
                     [("S", oev_S)], fwb_guard)

    tful = v32

    def emit_w3_fin(st, c_src, c_done):
        w3mm_T = {}
        w3mm_half = {}
        w3ev_V = {}
        acc_rl = {}
        for mp in range(NB // 2):
            wds = {}
            for hf in range(2):
                if mp == 0 and fwb_guard[0]:
                    wait(WD_ENG[hf], "T", fwb_guard[0])
                if mp >= 1:
                    wait(WD_ENG[hf], "T", w3mm_half[(mp - 1, hf)])
                wds[hf] = emit_h_wd(ws3, mp, hf)
            for mi in range(2):
                m = mp * 2 + mi
                if mp >= 1:
                    wait("sync", "V", w3ev_V[mp - 1])
                if st == 0 and mp == 0 and mi == 0:
                    wait("sync", "sp0", acc_sp_done[0])
                    wait("sync", "sp1", acc_sp_done[1])
                if st > 0 and mp == 0 and mi == 0:
                    wait("sync", "out0", cnt["out0"])
                    wait("sync", "out1", cnt["out1"])
                acc_rl[m] = op("sync", lambda e, m=m, mi=mi, st=st:
                               e.dma_start(out=FN[2 + mi][:, 0:SW],
                                           in_=acc_hbm[m][:, st * SW:
                                                          (st + 1) * SW]),
                               inc="sqa%d" % mi, n=16)
            pbank = pA if (mp % 2 == 0) else pB
            if mp == 0:
                sem, val = c_done
                wait("tensor", sem, val)
            if mp >= 2:
                wait("tensor", "V", w3ev_V[mp - 2])
            last = None
            for hf in range(2):
                for kb in range(16):
                    if kb % 8 == 0:
                        wait("tensor", "gq%d_%d" % (hf, kb // 8),
                             wds[hf][kb // 8])
                    kbg = hf * 16 + kb
                    for mi in range(2):
                        for n in range(0, SW, NW):
                            nwid = min(NW, SW - n)
                            mov = c_src(kbg)[:, n:n + nwid]
                            last = op(
                                "tensor",
                                lambda e, mi=mi, n=n, nwid=nwid, pbank=pbank,
                                mov=mov, kbg=kbg:
                                e.matmul(
                                    pbank[:, mi * PSB + n:mi * PSB + n + nwid],
                                    fwb[:, (kbg // 16) * 4096 + (kbg % 16) * 256
                                         + mi * 128:
                                         (kbg // 16) * 4096 + (kbg % 16) * 256
                                         + mi * 128 + 128],
                                    mov, start=(kbg == 0), stop=(kbg == 31)),
                                inc="T")
                w3mm_half[(mp, hf)] = last
            w3mm_T[mp] = last
            e_last = None
            wait("vector", "T", last)
            for mi in range(2):
                m = mp * 2 + mi
                s3_m = prm[:, PC_S3 + m:PC_S3 + m + 1]
                wait("vector", "sqa%d" % mi, acc_rl[m])
                e_last = op("vector",
                            lambda e, mi=mi, m=m, s3_m=s3_m, pbank=pbank:
                            e.scalar_tensor_tensor(
                                tful[:, m * SW:(m + 1) * SW],
                                pbank[:, mi * PSB:mi * PSB + SW], s3_m,
                                FN[2 + mi][:, 0:SW], op0=ALU.mult, op1=ALU.add),
                            inc="V")
            w3ev_V[mp] = e_last
        t_done_V = w3ev_V[NB // 2 - 1]
        fwb_guard[0] = w3mm_T[NB // 2 - 1]

        n_tc = SW // 128
        fin_V = {}
        out_cnt_l = {}
        for tc in range(n_tc):
            sl = tc % 2
            if tc == 0:
                wait("sync", "V", t_done_V)
            if tc >= 2:
                wait("sync", "V", fin_V[tc - 2])
            x_rl = op("sync", lambda e, tc=tc, st=st, sl=sl:
                      e.dma_start(out=FN[sl][:, 0:D],
                                  in_=xs[1 + st * SW + tc * 128:
                                         1 + st * SW + tc * 128 + 128, :]),
                      inc="sqf%d" % sl, n=16)
            pbank = pA if (tc % 2 == 0) else pB
            if tc == 0:
                wait("tensor", "V", t_done_V)
            if tc >= 2:
                wait("tensor", "V", fin_V[tc - 2])
            tl = None
            for m in range(NB):
                tl = op("tensor", lambda e, tc=tc, m=m, pbank=pbank:
                        e.transpose(pbank[:, m * 128:(m + 1) * 128],
                                    tful[:, m * SW + tc * 128:
                                         m * SW + tc * 128 + 128],
                                    idt[:, :]), inc="T")
            wait("vector", "T", tl)
            wait("vector", "sqf%d" % sl, x_rl)
            if tc >= 2:
                wait("vector", "out%d" % sl, out_cnt_l[tc - 2])
            fin_V[tc] = op("vector", lambda e, tc=tc, pbank=pbank, sl=sl:
                           e.tensor_tensor(FN[2 + sl][:, 0:D], pbank[:, 0:D],
                                           FN[sl][:, 0:D], op=ALU.add), inc="V")
            wait("sync", "V", fin_V[tc])
            out_cnt_l[tc] = op("sync", lambda e, tc=tc, st=st, sl=sl:
                               e.dma_start(out=y[st * SW + tc * 128:
                                                 st * SW + tc * 128 + 128, :],
                                           in_=FN[2 + sl][:, 0:D]),
                               inc="out%d" % sl, n=16)
        return fin_V[n_tc - 1], w3mm_T[NB // 2 - 1]

    if debug_dump:
        wait("sync", "S", h2_done[NHB // 2 - 1][1])
        op("sync", lambda e: e.dma_start(out=dbg_c, in_=abuf[:, :]),
           inc="sq", n=16)
        wait("sync", "sq", cnt["sq"])
    fin0_V, w3mm0_T = emit_w3_fin(0, lambda kb: abuf[:, kb * SW:(kb + 1) * SW],
                                  h2_done[NHB // 2 - 1])

    free("pool_b")
    xnT2 = alloc("xnT2", [128, NB * (T + 1)], BF16, side="right")
    wait("sync", "V", scan_done_V)
    wait("sync", "T", omm_T)
    xrl = op("sync", lambda e: e.dma_start(out=xnT2[:, :], in_=xn_hbm),
             inc="sqxr", n=16)
    wait("tensor", "sqxr", xrl)

    w12mm1_T, a1_done_V = emit_w12(
        1, lambda kb: xnT_b(kb, xnT2), [("V", fin0_V)],
        omm_T, w3mm0_T, lambda m: abuf[:, m * SW:(m + 1) * SW],
        lambda s: fwb[:, s * 4096:s * 4096 + 1024])

    fwb_guard[0] = w12mm1_T
    h1s1_done = emit_h(wsh1, lambda kb: abuf[:, kb * SW:(kb + 1) * SW],
                       gelu_evac(lambda m: b16[:, m * SW:(m + 1) * SW], PC_SH1),
                       [("V", a1_done_V)], fwb_guard)
    h2s1_done = emit_h(wsh2, lambda kb: b16[:, kb * SW:(kb + 1) * SW],
                       gelu_evac(lambda m: abuf[:, m * SW:(m + 1) * SW], PC_SH2),
                       [h1s1_done[NHB // 2 - 1]], fwb_guard)
    fin1_V, _ = emit_w3_fin(1, lambda kb: abuf[:, kb * SW:(kb + 1) * SW],
                            h2s1_done[NHB // 2 - 1])

    wait("sync", "out0", cnt["out0"])
    wait("sync", "out1", cnt["out1"])

    free("xnT2")
    free("finpool")
    free("fwb")

    from contextlib import ExitStack
    with ExitStack() as _ctx:
        for name in SEMNAMES:
            sems[name] = _ctx.enter_context(nc.semaphore("s_" + name))
        block = _ctx.enter_context(nc.Block())

        @block.sync
        def _(eng):
            for f in streams["sync"]:
                f(eng)

        @block.gpsimd
        def _(eng):
            for f in streams["gpsimd"]:
                f(eng)

        @block.vector
        def _(eng):
            for f in streams["vector"]:
                f(eng)

        @block.scalar
        def _(eng):
            for f in streams["scalar"]:
                f(eng)

        @block.tensor
        def _(eng):
            for f in streams["tensor"]:
                f(eng)

    psg[1].__exit__(None, None, None)
    psg[0].__exit__(None, None, None)
    for g in reversed(list(guards)):
        guards.pop(g).__exit__(None, None, None)
    return nc


def _host_prep(inputs, T=2048, ncore=NCORE):
    x = np.ascontiguousarray(np.asarray(inputs["L0_attn_x"], np.float32))
    B = x.shape[0]

    def colpack(v, nb):
        return np.ascontiguousarray(np.asarray(v, np.float32).reshape(nb, 128).T)

    pcol = np.zeros((128, PC_TOT), np.float32)
    pcol[:, PC_W:PC_W + 8] = colpack(inputs["L0_attn_time_decay"], 8)
    pcol[:, PC_U:PC_U + 8] = colpack(inputs["L0_attn_time_first"], 8)
    pcol[:, PC_TMK:PC_TMK + 8] = colpack(np.asarray(inputs["L0_attn_time_mix_k"]).ravel(), 8)
    pcol[:, PC_TMV:PC_TMV + 8] = colpack(np.asarray(inputs["L0_attn_time_mix_v"]).ravel(), 8)
    pcol[:, PC_TMR:PC_TMR + 8] = colpack(np.asarray(inputs["L0_attn_time_mix_r"]).ravel(), 8)
    pcol[:, PC_S3:PC_S3 + 8] = colpack(inputs["L0_ffn_w3_s"], 8)
    pcol[:, PC_S1:PC_S1 + 32] = colpack(inputs["L0_ffn_w1_s"], 32)
    pcol[:, PC_S2:PC_S2 + 32] = colpack(inputs["L0_ffn_w2_s"], 32)
    pcol[:, PC_SH1:PC_SH1 + 32] = colpack(inputs["L0_ffn_h1_s"], 32)
    pcol[:, PC_SH2:PC_SH2 + 32] = colpack(inputs["L0_ffn_h2_s"], 32)

    def tr(a):
        return np.ascontiguousarray(np.asarray(a, np.float32).T)

    weights = {
        "wkT": tr(inputs["L0_attn_W_k"]), "wvT": tr(inputs["L0_attn_W_v"]),
        "wrT": tr(inputs["L0_attn_W_r"]), "woT": tr(inputs["L0_attn_W_o"]),
        "w1T": tr(inputs["L0_ffn_w1_w"]), "w2T": tr(inputs["L0_ffn_w2_w"]),
        "h1T": tr(inputs["L0_ffn_h1_w"]), "h2T": tr(inputs["L0_ffn_h2_w"]),
        "w3T": tr(inputs["L0_ffn_w3_w"]),
        "ident": np.eye(128, dtype=np.float32),
        "pcol": pcol,
    }
    chunks_per_seq = ncore // B
    in_maps = []
    for c in range(ncore):
        bidx = c // chunks_per_seq
        j = c % chunks_per_seq
        t0 = j * T
        shard = np.empty((T + 1, D), np.float32)
        shard[1:] = x[bidx, t0:t0 + T]
        shard[0] = 0.0 if j == 0 else x[bidx, t0 - 1]
        cm = np.zeros((128, NGROUP), np.float32)
        cm[:, :j] = 1.0
        in_maps.append(dict(weights, xs=shard, cmask=cm))
    return in_maps


def kernel(**inputs):
    T = 2048
    in_maps = _host_prep(inputs, T)
    nc = build_nc(T=T, use_collective=True)
    res = run_bass_kernel_spmd(nc, in_maps, core_ids=list(range(NCORE)))
    x = inputs["L0_attn_x"]
    B, TT, _ = x.shape
    out = np.empty((B, TT, D), np.float32)
    chunks_per_seq = NCORE // B
    for c in range(NCORE):
        bidx = c // chunks_per_seq
        j = c % chunks_per_seq
        out[bidx, j * T:(j + 1) * T] = res.results[c]["y"]
    return out


# version 20 (final)
# speedup vs baseline: 1.0429x; 1.0429x over previous
import numpy as np
import concourse.bass as bass
import concourse.mybir as mybir
from concourse.bass_utils import run_bass_kernel_spmd

F32 = mybir.dt.float32
BF16 = mybir.dt.bfloat16
I8 = mybir.dt.int8
AF = mybir.ActivationFunctionType
ALU = mybir.AluOpType
AX = mybir.AxisListType

D = 1024
NB = D // 128
DH = 4096
NHB = DH // 128
SEG = 8
NCORE = 8
NGROUP = 4

PC_W, PC_U, PC_TMK, PC_TMV, PC_TMR, PC_S3 = 0, 8, 16, 24, 32, 40
PC_S1, PC_S2, PC_SH1, PC_SH2 = 48, 80, 112, 144
PC_TOT = 176


def build_nc(T=2048, use_collective=True, strict_sync=True, alias_sbuf=True,
             debug_dump=False):
    HALF = T // 2
    NSEGH = HALF // SEG
    NCHUNK = T // 128
    SW = T // 2
    PSB = max(SW, 512)
    NW = min(512, SW)

    nc = bass.Bass("TRN2", target_bir_lowering=False)

    xs = nc.dram_tensor("xs", [T + 1, D], F32, kind="ExternalInput").ap()
    pcol = nc.dram_tensor("pcol", [128, PC_TOT], F32, kind="ExternalInput").ap()
    cmask = nc.dram_tensor("cmask", [128, NGROUP], F32, kind="ExternalInput").ap()
    ident = nc.dram_tensor("ident", [128, 128], F32, kind="ExternalInput").ap()
    wk_d = nc.dram_tensor("wkT", [D, D], F32, kind="ExternalInput").ap()
    wv_d = nc.dram_tensor("wvT", [D, D], F32, kind="ExternalInput").ap()
    wr_d = nc.dram_tensor("wrT", [D, D], F32, kind="ExternalInput").ap()
    wo_d = nc.dram_tensor("woT", [D, D], F32, kind="ExternalInput").ap()
    w1_d = nc.dram_tensor("w1T", [D, DH], F32, kind="ExternalInput").ap()
    w2_d = nc.dram_tensor("w2T", [D, DH], F32, kind="ExternalInput").ap()
    h1_d = nc.dram_tensor("h1T", [DH, DH], F32, kind="ExternalInput").ap()
    h2_d = nc.dram_tensor("h2T", [DH, DH], F32, kind="ExternalInput").ap()
    w3_d = nc.dram_tensor("w3T", [DH, D], F32, kind="ExternalInput").ap()
    y = nc.dram_tensor("y", [T, D], F32, kind="ExternalOutput").ap()
    acc_hbm = nc.dram_tensor("acc_hbm", [NB, 128, T], F32, kind="Internal").ap()
    k_hbm = nc.dram_tensor("k_hbm", [NB, 128, T], F32, kind="Internal").ap()
    r_hbm = nc.dram_tensor("r_hbm", [NB, 128, T], BF16, kind="Internal").ap()
    b_hbm = nc.dram_tensor("b_hbm", [NHB // 2, 128, 2 * SW], BF16,
                           kind="Internal").ap()
    xn_hbm = nc.dram_tensor("xn_hbm", [128, NB * (T + 1)], BF16,
                            kind="Internal").ap()
    cc_in = nc.dram_tensor("cc_in", [128, 24], F32)
    cc_out = nc.dram_tensor("cc_out", [NGROUP * 128, 24], F32)
    if debug_dump:
        dbg_a = nc.dram_tensor("dbg_a", [128, NHB * SW], BF16,
                               kind="ExternalOutput").ap()
        dbg_wkv = nc.dram_tensor("dbg_wkv", [128, NB * T], F32,
                                 kind="ExternalOutput").ap()
        dbg_b = nc.dram_tensor("dbg_b", [128, NHB * SW], BF16,
                               kind="ExternalOutput").ap()
        dbg_c = nc.dram_tensor("dbg_c", [128, NHB * SW], BF16,
                               kind="ExternalOutput").ap()

    ENGS = ("sync", "gpsimd", "vector", "scalar", "tensor")
    streams = {e: [] for e in ENGS}
    SEMNAMES = ["sq", "sqx0", "sqx1", "gqw0", "gqw1", "gqf0", "gqf1",
                "gq0_0", "gq0_1", "gq0_2", "gq0_3",
                "gq1_0", "gq1_1", "gq1_2", "gq1_3",
                "sp0", "sp1", "spl", "spx", "sqc", "gqg", "sqk", "sqk1", "sqr", "sqr1", "sqb1", "cvh",
                "sqb", "sqa0", "sqa1", "sqf0", "sqf1", "sqxr",
                "out0", "out1", "cc", "T", "V", "S", "G"]
    cnt = {k: 0 for k in SEMNAMES}
    sems = {}
    STRICT_ENG = {"vector": "V", "scalar": "S"} if strict_sync else {}

    def op(eng, f, inc=None, n=1):
        if eng in STRICT_ENG:
            own = STRICT_ENG[eng]
            prev = cnt[own]
            if prev > 0:
                streams[eng].append(
                    lambda e, own=own, prev=prev: e.wait_ge(sems[own], prev))
            if inc is None:
                inc, n = own, 1
        if inc is None:
            streams[eng].append(lambda e, f=f: f(e))
            return None
        cnt[inc] += n
        if eng in STRICT_ENG and inc != STRICT_ENG[eng]:
            own = STRICT_ENG[eng]
            cnt[own] += 1
            streams[eng].append(
                lambda e, f=f, inc=inc, n=n, own=own:
                f(e).then_inc(sems[inc], n).then_inc(sems[own], 1))
        else:
            streams[eng].append(
                lambda e, f=f, inc=inc, n=n: f(e).then_inc(sems[inc], n))
        return cnt[inc]

    def wait(eng, sem, val):
        if val is None or val <= 0:
            return
        streams[eng].append(lambda e, sem=sem, val=val: e.wait_ge(sems[sem], val))

    guards = {}

    def alloc(name, shape, dtype, side="left"):
        g = nc.sbuf_tensor(name, shape, dtype, side=side)
        t = g.__enter__()
        guards[name] = g
        return t

    def free(*names):
        if not alias_sbuf:
            return
        for n in names:
            guards.pop(n).__exit__(None, None, None)

    v32 = alloc("v32", [128, NB * T], F32)
    abuf = alloc("abuf", [128, NHB * SW], BF16)
    wbuf = alloc("wbuf", [128, 2 * 1024], BF16)
    fp2 = alloc("fp2", [128, 2 * 1024], F32)
    prm = alloc("prm", [128, PC_TOT], F32)
    dpr = alloc("dpr", [128, 56], F32)
    smb = alloc("smb", [128, 24], F32)
    gat = alloc("gat", [128, NGROUP * 24], F32)
    carr = alloc("carr", [128, 48], F32)
    idt = alloc("idt", [128, 128], F32)
    io7 = alloc("io7", [128, HALF], I8)
    stt6 = alloc("stt6", [128, 16], F32)
    cmk = alloc("cmk", [128, NGROUP], F32)
    xnT = alloc("xnT", [128, NB * (T + 1)], BF16, side="right")

    d_expw = dpr[:, 0:8]
    d_expu = dpr[:, 8:16]
    d_w8 = dpr[:, 16:24]
    d_w7 = dpr[:, 24:32]
    d_wT = dpr[:, 32:40]
    d_eps = dpr[:, 40:41]
    d_wneg = dpr[:, 48:56]

    FPs = [fp2[:, 0:1024], fp2[:, 1024:2048]]
    FPs16 = [fp2[:, 0:1024].bitcast(BF16), fp2[:, 1024:2048].bitcast(BF16)]

    def xnT_b(b, t=None):
        ap = xnT if t is None else t
        return ap[:, b * (T + 1):(b + 1) * (T + 1)]

    def v_b(b):
        return v32[:, b * T:(b + 1) * T]

    psg = [nc.psum_tensor("pA", [128, 2048], F32),
           nc.psum_tensor("pB", [128, 2048], F32)]
    pA = psg[0].__enter__()
    pB = psg[1].__enter__()

    def seg3(ap2d):
        return ap2d.rearrange("p (s c) -> p s c", c=SEG)

    op("sync", lambda e: e.dma_start(out=prm[:, :], in_=pcol), inc="sq", n=16)
    op("sync", lambda e: e.dma_start(out=idt[:, :], in_=ident), inc="sq", n=16)
    op("sync", lambda e: e.dma_start(out=cmk[:, :], in_=cmask), inc="sq", n=16)
    ld_const = cnt["sq"]
    g_iota = op("gpsimd", lambda e: e.iota(
        io7[:, :], pattern=[[0, NSEGH], [1, SEG]], base=0, channel_multiplier=0,
        allow_small_or_imprecise_dtypes=True), inc="G")


    wait("scalar", "sq", ld_const)
    op("scalar", lambda e: e.activation(d_expw, prm[:, PC_W:PC_W + 8], AF.Exp))
    op("scalar", lambda e: e.activation(d_expu, prm[:, PC_U:PC_U + 8], AF.Exp))
    wait("vector", "sq", ld_const)
    op("vector", lambda e: e.memset(d_eps, 1e-5))
    op("vector", lambda e: e.tensor_scalar_mul(d_w8, prm[:, PC_W:PC_W + 8], 8.0))
    op("vector", lambda e: e.tensor_scalar_mul(d_w7, prm[:, PC_W:PC_W + 8], 7.0))
    op("vector", lambda e: e.tensor_scalar_mul(d_wT, prm[:, PC_W:PC_W + 8],
                                               float(T)))
    op("vector", lambda e: e.tensor_scalar_mul(d_wneg, prm[:, PC_W:PC_W + 8],
                                               -1.0), inc="V")

    xb = alloc("xb", [128, 2 * D], F32, side="right")
    nchunks_x = NCHUNK + 1
    tp_T = {}
    ev_S = {}
    for c in range(nchunks_x):
        rows = 128 if c < NCHUNK else 1
        slot = c % 2
        xsl = xb[:rows, slot * D:slot * D + D]
        if c >= 2:
            wait("sync", "T", tp_T[c - 2])
        sq_c = op("sync", lambda e, c=c, xsl=xsl, rows=rows:
                  e.dma_start(out=xsl, in_=xs[c * 128:c * 128 + rows, :]),
                  inc="sqx%d" % slot, n=16)
        wait("vector", "sqx%d" % slot, sq_c)
        op("vector", lambda e, xsl=xsl, rows=rows:
           e.bn_stats(stt6[:rows, 0:6], xsl[:, 0:512]))
        op("vector", lambda e, xsl=xsl, rows=rows:
           e.bn_stats(stt6[:rows, 6:12], xsl[:, 512:1024]))
        vst = op("vector", lambda e, rows=rows:
                 e.bn_aggr(stt6[:rows, 12:14], stt6[:rows, 0:12]), inc="V")
        wait("scalar", "V", vst)
        ssc = op("scalar", lambda e, rows=rows:
                 e.activation(stt6[:rows, 14:15], stt6[:rows, 13:14], AF.Ln,
                              bias=d_eps[:rows, :]), inc="S")
        wait("scalar", "S", ssc)
        ssc2 = op("scalar", lambda e, rows=rows:
                  e.activation(stt6[:rows, 15:16], stt6[:rows, 14:15], AF.Exp,
                               scale=-0.5), inc="S")
        wait("vector", "S", ssc2)
        vn = op("vector", lambda e, xsl=xsl, rows=rows:
                e.tensor_scalar(xsl, xsl, stt6[:rows, 12:13], stt6[:rows, 15:16],
                                op0=ALU.subtract, op1=ALU.mult), inc="V")
        pbank = pA if (c % 2 == 0) else pB
        wait("tensor", "V", vn)
        if c >= 2:
            wait("tensor", "S", ev_S[c - 2])
        tl = None
        for fb in range(NB):
            tl = op("tensor", lambda e, xsl=xsl, fb=fb, pbank=pbank, rows=rows:
                    e.transpose(pbank[:, fb * 128:fb * 128 + rows],
                                xsl[:, fb * 128:(fb + 1) * 128],
                                idt[:rows, :rows]), inc="T")
        tp_T[c] = tl
        wait("scalar", "T", tl)
        se = None
        for fb in range(NB):
            se = op("scalar", lambda e, c=c, fb=fb, pbank=pbank, rows=rows:
                    e.copy(xnT[:, fb * (T + 1) + c * 128:
                               fb * (T + 1) + c * 128 + rows],
                           pbank[:, fb * 128:fb * 128 + rows]), inc="S")
        ev_S[c] = se
    xnT_done_S = ev_S[nchunks_x - 1]
    free("xb")

    wait("sync", "S", xnT_done_S)
    spx_c = op("sync", lambda e: e.dma_start(out=xn_hbm, in_=xnT[:, :]),
               inc="spx", n=16)

    ws1 = w1_d.rearrange("(kb p) o -> p kb o", p=128)
    ws2 = w2_d.rearrange("(kb p) o -> p kb o", p=128)
    wsh1 = h1_d.rearrange("(kb p) o -> p kb o", p=128)
    wsh2 = h2_d.rearrange("(kb p) o -> p kb o", p=128)
    ws3 = w3_d.rearrange("(kb p) o -> p kb o", p=128)
    spill_state = {"idx": 0, "last_dma": {0: 0, 1: 0}}

    def emit_w12(st, base_xnT, psum_guard0, wbuf_guard_T, fwb_guard_T, a_dst,
                 w2buf):
        base = 1 + st * SW
        wmm_T = {}
        wev_V = {}
        for m in range(NHB):
            if m < 2 and wbuf_guard_T:
                wait("gpsimd", "T", wbuf_guard_T)
            if m < 2 and fwb_guard_T:
                wait("gpsimd", "T", fwb_guard_T)
            if m >= 2:
                wait("gpsimd", "T", wmm_T[m - 2])
            wd1 = op("gpsimd", lambda e, m=m:
                     e.dma_start(out=wbuf[:, (m % 2) * 1024:(m % 2) * 1024 + 1024]
                                 .rearrange("p (kb o) -> p kb o", o=128),
                                 in_=ws1[:, :, m * 128:(m + 1) * 128]),
                     inc="gqw%d" % (m % 2), n=16)
            w2slot = w2buf((m % 2))
            wd2 = op("gpsimd", lambda e, m=m, w2slot=w2slot:
                     e.dma_start(out=w2slot
                                 .rearrange("p (kb o) -> p kb o", o=128),
                                 in_=ws2[:, :, m * 128:(m + 1) * 128]),
                     inc="gqf%d" % (m % 2), n=16)
            wait("tensor", "gqw%d" % (m % 2), wd1)
            wait("tensor", "gqf%d" % (m % 2), wd2)
            if m == 0:
                for sem, val in psum_guard0:
                    wait("tensor", sem, val)
            pbank = pA if (m % 2 == 0) else pB
            if m >= 2:
                wait("tensor", "V", wev_V[m - 2])
            last = None
            for n in range(0, SW, NW):
                nwid = min(NW, SW - n)
                for kb in range(NB):
                    mov = base_xnT(kb)[:, base + n:base + n + nwid]
                    op("tensor",
                       lambda e, m=m, kb=kb, pbank=pbank, nwid=nwid, n=n, mov=mov:
                       e.matmul(pbank[:, n:n + nwid],
                                wbuf[:, (m % 2) * 1024 + kb * 128:
                                     (m % 2) * 1024 + kb * 128 + 128],
                                mov, start=(kb == 0), stop=(kb == NB - 1)),
                       inc="T")
            for n in range(0, SW, NW):
                nwid = min(NW, SW - n)
                for kb in range(NB):
                    mov = base_xnT(kb)[:, base + n:base + n + nwid]
                    last = op("tensor",
                              lambda e, m=m, kb=kb, pbank=pbank, nwid=nwid, n=n,
                              mov=mov, w2slot=w2slot:
                              e.matmul(pbank[:, PSB + n:PSB + n + nwid],
                                       w2slot[:, kb * 128:kb * 128 + 128],
                                       mov, start=(kb == 0), stop=(kb == NB - 1)),
                              inc="T")
            wmm_T[m] = last
            s1_m = prm[:, PC_S1 + m:PC_S1 + m + 1]
            s2_m = prm[:, PC_S2 + m:PC_S2 + m + 1]
            wait("scalar", "T", last)
            if m == 0:
                wait("scalar", "sp0", spill_state["last_dma"][0])
                wait("scalar", "sp1", spill_state["last_dma"][1])
            if m >= 1:
                wait("scalar", "V", wev_V[m - 1])
            ssg = op("scalar", lambda e, pbank=pbank, s1_m=s1_m:
                     e.activation(FPs[0][:, 0:SW], pbank[:, 0:SW], AF.Sigmoid,
                                  scale=s1_m), inc="S")
            wait("vector", "S", ssg)
            op("vector", lambda e, pbank=pbank, s2_m=s2_m:
               e.scalar_tensor_tensor(FPs[1][:, 0:SW], pbank[:, PSB:PSB + SW],
                                      s2_m, FPs[0][:, 0:SW],
                                      op0=ALU.mult, op1=ALU.mult))
            dap = a_dst(m)
            wev_V[m] = op("vector", lambda e, pbank=pbank, s1_m=s1_m, dap=dap:
                          e.scalar_tensor_tensor(dap, pbank[:, 0:SW], s1_m,
                                                 FPs[1][:, 0:SW],
                                                 op0=ALU.mult, op1=ALU.mult),
                          inc="V")
        return wmm_T[NHB - 1], wev_V[NHB - 1]

    w2b = alloc("w2b", [128, 2 * 1024], BF16, side="right")
    w12mm_T, a_done_V = emit_w12(0, lambda kb: xnT_b(kb),
                                 [("S", xnT_done_S)], None,
                                 tp_T[nchunks_x - 1],
                                 lambda m: abuf[:, m * SW:(m + 1) * SW],
                                 lambda s: w2b[:, s * 1024:(s + 1) * 1024])
    free("w2b")

    xmix = alloc("xmix", [128, NB * T], BF16, side="right")

    def emit_lerp(pc_off, pre_waits=()):
        for sem, val in pre_waits:
            wait("vector", sem, val)
        last = None
        for b in range(NB):
            tm = prm[:, pc_off + b:pc_off + b + 1]
            for h in range(2):
                cur = xnT_b(b)[:, h * HALF + 1:h * HALF + 1 + HALF]
                prv = xnT_b(b)[:, h * HALF:h * HALF + HALF]
                dst = xmix[:, b * T + h * HALF:b * T + h * HALF + HALF]
                op("vector", lambda e, cur=cur, prv=prv, dst=dst:
                   e.tensor_tensor(dst, cur, prv, op=ALU.subtract))
                last = op("vector", lambda e, dst=dst, prv=prv, tm=tm:
                          e.scalar_tensor_tensor(dst, dst, tm, prv,
                                                 op0=ALU.mult, op1=ALU.add),
                          inc="V")
        return last

    def emit_proj(w_dram, src, lerp_V, psum_guards, evac, wbuf_guard=None,
                  interleave=None, psum_guards1=()):
        wsrc = w_dram.rearrange("(kb p) o -> p kb o", p=128)
        mm_T = {}
        evc = {}
        ev_sem = None
        for m in range(NB):
            if m < 2 and wbuf_guard:
                wait("gpsimd", "T", wbuf_guard)
            if m >= 2:
                wait("gpsimd", "T", mm_T[m - 2])
            wdma = op("gpsimd", lambda e, m=m, wsrc=wsrc:
                      e.dma_start(out=wbuf[:, (m % 2) * 1024:(m % 2) * 1024 + 1024]
                                  .rearrange("p (kb o) -> p kb o", o=128),
                                  in_=wsrc[:, :, m * 128:(m + 1) * 128]),
                      inc="gqw%d" % (m % 2), n=16)
            wait("tensor", "gqw%d" % (m % 2), wdma)
            if m == 0:
                wait("tensor", "V", lerp_V)
                for sem, val in psum_guards:
                    wait("tensor", sem, val)
            if m == 1:
                for sem, val in psum_guards1:
                    wait("tensor", sem, val)
            pbank = pA if (m % 2 == 0) else pB
            if m >= 2:
                wait("tensor", ev_sem, evc[m - 2])
            last = None
            for n in range(0, T, 512):
                nwid = min(512, T - n)
                for kb in range(NB):
                    mov = src(kb)[:, n:n + nwid]
                    last = op("tensor",
                              lambda e, m=m, kb=kb, pbank=pbank, nwid=nwid,
                              n=n, mov=mov:
                              e.matmul(pbank[:, n:n + nwid],
                                       wbuf[:, (m % 2) * 1024 + kb * 128:
                                            (m % 2) * 1024 + kb * 128 + 128],
                                       mov,
                                       start=(kb == 0), stop=(kb == NB - 1)),
                              inc="T")
            mm_T[m] = last
            ev_sem, evc[m] = evac(m, pbank, last)
            if interleave is not None:
                interleave(m)
        return mm_T[NB - 1], evc[NB - 1]

    def spill_evac_f32(dst_hbm):
        def evac(m, pbank, mm_last):
            wait("scalar", "T", mm_last)
            last = None
            for off in range(0, T, 1024):
                w = min(1024, T - off)
                sidx = spill_state["idx"] % 2
                spill_state["idx"] += 1
                prev_dma = spill_state["last_dma"][sidx]
                if prev_dma:
                    wait("scalar", "sp%d" % sidx, prev_dma)
                for n2 in range(0, w, 512):
                    n2w = min(512, w - n2)
                    last = op("scalar",
                              lambda e, off=off, n2=n2, pbank=pbank, sidx=sidx,
                              n2w=n2w:
                              e.copy(FPs[sidx][:, n2:n2 + n2w],
                                     pbank[:, off + n2:off + n2 + n2w]),
                              inc="S")
                wait("sync", "S", last)
                dmac = op("sync", lambda e, m=m, off=off, sidx=sidx, w=w,
                          dst_hbm=dst_hbm:
                          e.dma_start(out=dst_hbm[m][:, off:off + w],
                                      in_=FPs[sidx][:, 0:w]),
                          inc="sp%d" % sidx, n=16)
                spill_state["last_dma"][sidx] = dmac
            return "S", last
        return evac

    def spill_evac_sigmoid16(dst_hbm):
        def evac(m, pbank, mm_last):
            wait("scalar", "T", mm_last)
            sidx = spill_state["idx"] % 2
            spill_state["idx"] += 1
            prev_dma = spill_state["last_dma"][sidx]
            if prev_dma:
                wait("scalar", "sp%d" % sidx, prev_dma)
            last = None
            for n in range(0, T, 512):
                nwid = min(512, T - n)
                last = op("scalar", lambda e, n=n, pbank=pbank, sidx=sidx,
                          nwid=nwid:
                          e.activation(FPs16[sidx][:, n:n + nwid],
                                       pbank[:, n:n + nwid], AF.Sigmoid),
                          inc="S")
            wait("sync", "S", last)
            dmac = op("sync", lambda e, m=m, sidx=sidx, dst_hbm=dst_hbm:
                      e.dma_start(out=dst_hbm[m], in_=FPs16[sidx][:, 0:T]),
                      inc="sp%d" % sidx, n=16)
            spill_state["last_dma"][sidx] = dmac
            return "S", last
        return evac

    def copy_evac(dst, rec=None):
        def evac(m, pbank, mm_last):
            wait("scalar", "T", mm_last)
            last = None
            for n in range(0, T, 512):
                nwid = min(512, T - n)
                dap = dst(m)[:, n:n + nwid]
                last = op("scalar", lambda e, pbank=pbank, nwid=nwid, n=n, dap=dap:
                          e.copy(dap, pbank[:, n:n + nwid]), inc="S")
            if rec is not None:
                rec[m] = last
            return "S", last
        return evac

    def xmix_src(kb):
        return xmix[:, kb * T:(kb + 1) * T]

    lerpr_V = emit_lerp(PC_TMR, pre_waits=[("S", xnT_done_S)])
    wait("scalar", "V", a_done_V)
    rmm_T, rev_S = emit_proj(wr_d, xmix_src, lerpr_V,
                             [("S", xnT_done_S), ("V", a_done_V)],
                             spill_evac_sigmoid16(r_hbm), wbuf_guard=w12mm_T)
    lerpk_V = emit_lerp(PC_TMK, pre_waits=[("T", rmm_T)])
    kmm_T, kev_S = emit_proj(wk_d, xmix_src, lerpk_V,
                             [("S", rev_S)], spill_evac_f32(k_hbm),
                             wbuf_guard=rmm_T)
    lerpv_V = emit_lerp(PC_TMV, pre_waits=[
        ("T", kmm_T), ("sp0", spill_state["last_dma"][0]),
        ("sp1", spill_state["last_dma"][1])])
    k_spills_done = dict(spill_state["last_dma"])

    free("xmix")
    free("xnT")
    PW = HALF + 8
    pool_a = alloc("pool_a", [128, 4 * PW], F32, side="right")
    kstr = alloc("kstr", [128, HALF], F32, side="right")
    CW = NSEGH + 1
    cpa = alloc("cpa", [128, 12 * CW], F32, side="right")
    hst = alloc("hst", [128, 8], F32, side="right")
    rbb = alloc("rbb", [128, T], BF16, side="right")
    C = [cpa[:, i * CW:(i + 1) * CW] for i in range(12)]
    GB = C[11]

    kload_cnt = {"n": 0}
    kstr_guard_V = [0]

    def emit_kload(b, h):
        n = kload_cnt["n"]
        kload_cnt["n"] += 1
        if n < 2:
            wait("sync", "sp0", k_spills_done[0])
            wait("sync", "sp1", k_spills_done[1])
            return op("sync", lambda e, h=h:
                      e.dma_start(out=FPs[h][:, 0:HALF],
                                  in_=k_hbm[0][:, h * HALF:(h + 1) * HALF]),
                      inc="sqk" if n == 0 else "sqk1", n=16)
        wait("sync", "V", kstr_guard_V[0])
        if n == 2:
            wait("sync", "sqk", 16)
            wait("sync", "spx", spx_c)
            wait("sync", "V", lerpv_V)
        return op("sync", lambda e, b=b, h=h:
                  e.dma_start(out=kstr[:, :],
                              in_=k_hbm[b][:, h * HALF:(h + 1) * HALF]),
                  inc="sqk", n=16)

    S = [pool_a[:, i * PW:(i + 1) * PW] for i in range(4)]
    v32b16 = v32[:, :].bitcast(BF16)

    def rw16_b(b):
        return v32b16[:, b * 2 * T:b * 2 * T + T]

    RB = [rbb[:, 0:HALF], rbb[:, HALF:2 * HALF]]
    v_ev = {}
    p1_state = {"V": None}

    def emit_p1_block(b):
        if b == 0:
            wait("vector", "G", g_iota)
            wait("vector", "spx", spx_c)
            wait("vector", "V", lerpv_V)
        w_b = prm[:, PC_W + b:PC_W + b + 1]
        w8_b = d_w8[:, b:b + 1]
        w7_b = d_w7[:, b:b + 1]
        for h in range(2):
            kl = emit_kload(b, h)
            ks = FPs[h] if b == 0 else kstr
            vs = v_b(b)[:, h * HALF:(h + 1) * HALF]
            wn_b = d_wneg[:, b:b + 1]
            wait("vector", "sqk1" if (b == 0 and h == 1) else "sqk", kl)
            kdone = op("vector", lambda e, wn_b=wn_b, ks=ks:
                       e.scalar_tensor_tensor(
                           S[1][:, 0:HALF], io7[:, :], wn_b, ks[:, 0:HALF],
                           op0=ALU.mult, op1=ALU.add), inc="V")
            if b > 0:
                kstr_guard_V[0] = kdone
            op("vector", lambda e: e.tensor_reduce(
                C[0][:, 0:NSEGH], seg3(S[1][:, 0:HALF]), axis=AX.X, op=ALU.max))
            vv = op("vector", lambda e: e.tensor_tensor(
                seg3(S[2][:, 0:HALF]), seg3(S[1][:, 0:HALF]),
                C[0][:, 0:NSEGH].unsqueeze(2).broadcast_to([128, NSEGH, SEG]),
                op=ALU.subtract), inc="V")
            wait("scalar", "V", vv)
            sv = op("scalar", lambda e: e.activation(
                S[2][:, 0:HALF], S[2][:, 0:HALF], AF.Exp), inc="S")
            op("vector", lambda e, w7_b=w7_b: e.tensor_scalar_add(
                C[3][:, 0:NSEGH], C[0][:, 0:NSEGH], w7_b))
            if h == 0:
                op("vector", lambda e: e.memset(C[4][:, 0:1], 0.0))
            else:
                op("vector", lambda e: e.tensor_copy(C[4][:, 0:1], hst[:, 2:3]))
            op("vector", lambda e, w8_b=w8_b: e.tensor_tensor_scan(
                C[4][:, 1:1 + NSEGH], w8_b.broadcast_to([128, NSEGH]),
                C[3][:, 0:NSEGH], initial=C[4][:, 0:1],
                op0=ALU.add, op1=ALU.max))
            op("vector", lambda e: e.tensor_tensor(
                C[5][:, 0:NSEGH], C[4][:, 0:NSEGH], C[4][:, 1:1 + NSEGH],
                op=ALU.subtract))
            vv = op("vector", lambda e: e.tensor_tensor(
                C[6][:, 0:NSEGH], C[3][:, 0:NSEGH], C[4][:, 1:1 + NSEGH],
                op=ALU.subtract), inc="V")
            wait("scalar", "S", sv)
            wait("scalar", "V", vv)
            op("scalar", lambda e, w8_b=w8_b: e.activation(
                C[5][:, 0:NSEGH], C[5][:, 0:NSEGH], AF.Exp, bias=w8_b))
            sv2 = op("scalar", lambda e: e.activation(
                C[6][:, 0:NSEGH], C[6][:, 0:NSEGH], AF.Exp), inc="S")
            wait("vector", "S", sv)
            wait("vector", "S", v_ev[b])
            op("vector", lambda e, vs=vs: e.tensor_tensor(
                S[3][:, 0:HALF], S[2][:, 0:HALF], vs, op=ALU.mult))
            op("vector", lambda e: e.tensor_reduce(
                C[1][:, 0:NSEGH], seg3(S[3][:, 0:HALF]), axis=AX.X, op=ALU.add))
            op("vector", lambda e: e.tensor_reduce(
                C[2][:, 0:NSEGH], seg3(S[2][:, 0:HALF]), axis=AX.X, op=ALU.add))
            wait("vector", "S", sv2)
            op("vector", lambda e: e.tensor_mul(
                C[7][:, 0:NSEGH], C[6][:, 0:NSEGH], C[1][:, 0:NSEGH]))
            op("vector", lambda e: e.tensor_mul(
                C[8][:, 0:NSEGH], C[6][:, 0:NSEGH], C[2][:, 0:NSEGH]))
            if h == 0:
                op("vector", lambda e: e.memset(C[9][:, 0:1], 0.0))
                op("vector", lambda e: e.memset(C[10][:, 0:1], 0.0))
            else:
                op("vector", lambda e: e.tensor_copy(C[9][:, 0:1], hst[:, 0:1]))
                op("vector", lambda e: e.tensor_copy(C[10][:, 0:1], hst[:, 1:2]))
            op("vector", lambda e: e.tensor_tensor_scan(
                C[9][:, 1:1 + NSEGH], C[5][:, 0:NSEGH], C[7][:, 0:NSEGH],
                initial=C[9][:, 0:1], op0=ALU.mult, op1=ALU.add))
            op("vector", lambda e: e.tensor_tensor_scan(
                C[10][:, 1:1 + NSEGH], C[5][:, 0:NSEGH], C[8][:, 0:NSEGH],
                initial=C[10][:, 0:1], op0=ALU.mult, op1=ALU.add))
            op("vector", lambda e: e.tensor_copy(hst[:, 0:1],
                                                 C[9][:, NSEGH:NSEGH + 1]))
            op("vector", lambda e: e.tensor_copy(hst[:, 1:2],
                                                 C[10][:, NSEGH:NSEGH + 1]))
            op("vector", lambda e: e.tensor_copy(hst[:, 2:3],
                                                 C[4][:, NSEGH:NSEGH + 1]))
        op("vector", lambda e, b=b: e.tensor_copy(smb[:, b:b + 1], hst[:, 0:1]))
        op("vector", lambda e, b=b: e.tensor_copy(smb[:, 8 + b:9 + b], hst[:, 1:2]))
        p1_state["V"] = op("vector", lambda e, b=b: e.tensor_copy(
            smb[:, 16 + b:17 + b], hst[:, 2:3]), inc="V")

    P1_LAG = 3
    vmm_T, vev_S = emit_proj(wv_d, xmix_src, lerpv_V,
                             [("S", kev_S)], copy_evac(v_b, rec=v_ev),
                             wbuf_guard=kmm_T,
                             interleave=lambda m: (emit_p1_block(m - P1_LAG)
                                                   if m >= P1_LAG else None))

    WD_ENG = {0: "gpsimd", 1: "gpsimd"}

    def emit_h_wd_q(wsrc, mp, hf, q):
        return [op(WD_ENG[hf], lambda e, mp=mp, hf=hf, q=q, wsrc=wsrc:
                e.dma_start(
                    out=fwb[:, hf * 4096 + q * 2048:hf * 4096 + (q + 1) * 2048]
                    .rearrange("p (kb o) -> p kb o", o=256),
                    in_=wsrc[:, hf * 16 + q * 8:hf * 16 + (q + 1) * 8,
                             mp * 256:(mp + 1) * 256]),
                inc="gq%d_%d" % (hf, q), n=16)]

    def emit_h_wd(wsrc, mp, hf):
        return emit_h_wd_q(wsrc, mp, hf, 0) + emit_h_wd_q(wsrc, mp, hf, 1)

    def emit_h(wsrc, src16, dst_evac, first_guards, fwb_guard, xn_guard=False,
               interleave=None, preload=None, inject=None, pre0_rbb=None):
        hmm_T = {}
        hmm_half = {}
        hmm_q = {}
        hev = {}
        for mp in range(NHB // 2):
            if mp == 0 and preload is not None:
                wds = preload
            else:
                wds = {}
                for hf in range(2):
                    eng = WD_ENG[hf]
                    if mp == 0 and fwb_guard[0]:
                        wait(eng, "T", fwb_guard[0])
                    if mp == 0 and xn_guard:
                        wait(eng, "spx", spx_c)
                        wait(eng, "T", vmm_T)
                        wait(eng, "V", lerpv_V)
                    if mp >= 1:
                        wds[hf] = []
                        for q in range(2):
                            wait(eng, "T", hmm_q[(mp - 1, hf, q)])
                            wds[hf] += emit_h_wd_q(wsrc, mp, hf, q)
                    else:
                        wds[hf] = emit_h_wd(wsrc, mp, hf)
            if inject is not None and mp in inject:
                inject[mp]()
            pbank = pA if (mp % 2 == 0) else pB
            if mp == 0:
                for sem, val in first_guards:
                    wait("tensor", sem, val)
            if mp >= 2:
                sem, val = hev[mp - 2]
                wait("tensor", sem, val)
            last = None
            for hf in range(2):
                for kb in range(16):
                    if kb % 8 == 0:
                        wait("tensor", "gq%d_%d" % (hf, kb // 8),
                             wds[hf][kb // 8])
                    kbg = hf * 16 + kb
                    for mi in range(2):
                        if (mp == 0 and hf == 0 and kb < 8
                                and pre0_rbb is not None):
                            stat = pre0_rbb[:, kb * 256 + mi * 128:
                                            kb * 256 + mi * 128 + 128]
                        else:
                            stat = fwb[:, hf * 4096 + kb * 256 + mi * 128:
                                       hf * 4096 + kb * 256 + mi * 128 + 128]
                        for n in range(0, SW, NW):
                            nwid = min(NW, SW - n)
                            mov = src16(kbg)[:, n:n + nwid]
                            last = op(
                                "tensor",
                                lambda e, mi=mi, n=n, nwid=nwid,
                                pbank=pbank, mov=mov, kbg=kbg, stat=stat:
                                e.matmul(
                                    pbank[:, mi * PSB + n:mi * PSB + n + nwid],
                                    stat,
                                    mov, start=(kbg == 0), stop=(kbg == 31)),
                                inc="T")
                    if kb % 8 == 7:
                        hmm_q[(mp, hf, kb // 8)] = last
                hmm_half[(mp, hf)] = last
            hmm_T[mp] = last
            hev[mp] = dst_evac(mp, pbank, last)
            if interleave is not None:
                interleave(mp)
        fwb_guard[0] = hmm_T[NHB // 2 - 1]
        return hev

    wait("gpsimd", "spx", spx_c)
    wait("gpsimd", "V", lerpv_V)
    if T >= 2048:
        pre0 = op("gpsimd", lambda e: e.dma_start(
            out=rbb[:, :].rearrange("p (kb o) -> p kb o", o=256),
            in_=wsh1[:, 0:8, 0:256]), inc="gq0_0", n=16)
        wait("gpsimd", "T", vmm_T)
        h1_pre = {0: [pre0] + emit_h_wd_q(wsh1, 0, 0, 1),
                  1: emit_h_wd(wsh1, 0, 1)}
        h1_rbb = rbb
    else:
        wait("gpsimd", "T", vmm_T)
        h1_pre = {0: emit_h_wd(wsh1, 0, 0), 1: emit_h_wd(wsh1, 0, 1)}
        h1_rbb = None

    exch = {"cfin": None, "carry_V": None, "sq1": None}
    if use_collective:
        def emit_sq1():
            wait("sync", "V", p1_state["V"])
            exch["sq1"] = op("sync", lambda e:
                             e.dma_start(out=cc_in.ap(), in_=smb[:, :]),
                             inc="sqc", n=16)

        def emit_exchange():
            wait("gpsimd", "sqc", exch["sq1"])
            ccv = op("gpsimd", lambda e: e.collective_compute(
                "AllGather", ALU.bypass,
                replica_groups=[[0, 1, 2, 3], [4, 5, 6, 7]],
                ins=[cc_in.ap().opt()], outs=[cc_out.ap().opt()]), inc="cc", n=1)
            wait("gpsimd", "cc", ccv)
            gq1 = op("gpsimd", lambda e: e.dma_start(
                out=gat[:, :].rearrange("p (r c) -> p r c", c=24),
                in_=cc_out.ap().rearrange("(r p) c -> p r c", p=128)),
                inc="gqg", n=16)
            wait("vector", "gqg", gq1)
            cA, cB = carr[:, 0:24], carr[:, 24:48]
            op("vector", lambda e: e.memset(cA, 0.0))
            carry_V = None
            for r in range(NGROUP - 1):
                src_, dst = (cA, cB) if r % 2 == 0 else (cB, cA)
                m_r = cmk[:, r:r + 1]
                ga = gat[:, r * 24:r * 24 + 8]
                gb = gat[:, r * 24 + 8:r * 24 + 16]
                gq_ = gat[:, r * 24 + 16:r * 24 + 24]
                t_a, t_b, t_q, t_d, u1 = (C[0][:, 0:8], C[1][:, 0:8], C[2][:, 0:8],
                                          C[3][:, 0:8], C[4][:, 0:8])
                e1, e2 = C[5][:, 0:8], C[6][:, 0:8]
                op("vector", lambda e, ga=ga, m_r=m_r, t_a=t_a:
                   e.tensor_scalar_mul(t_a, ga, m_r))
                op("vector", lambda e, gb=gb, m_r=m_r, t_b=t_b:
                   e.tensor_scalar_mul(t_b, gb, m_r))
                op("vector", lambda e, gq_=gq_, m_r=m_r, t_q=t_q:
                   e.tensor_scalar_mul(t_q, gq_, m_r))
                op("vector", lambda e, m_r=m_r, t_d=t_d:
                   e.tensor_scalar_mul(t_d, d_wT, m_r))
                op("vector", lambda e, src_=src_, t_d=t_d, u1=u1:
                   e.tensor_add(u1, src_[:, 16:24], t_d))
                op("vector", lambda e, dst=dst, u1=u1, t_q=t_q:
                   e.tensor_max(dst[:, 16:24], u1, t_q))
                op("vector", lambda e, dst=dst, u1=u1, e1=e1:
                   e.tensor_tensor(e1, u1, dst[:, 16:24], op=ALU.subtract))
                vv = op("vector", lambda e, dst=dst, t_q=t_q, e2=e2:
                        e.tensor_tensor(e2, t_q, dst[:, 16:24], op=ALU.subtract),
                        inc="V")
                wait("scalar", "V", vv)
                op("scalar", lambda e, e1=e1: e.activation(e1, e1, AF.Exp))
                sv = op("scalar", lambda e, e2=e2: e.activation(e2, e2, AF.Exp),
                        inc="S")
                wait("vector", "S", sv)
                op("vector", lambda e, src_=src_, e1=e1, u1=u1:
                   e.tensor_mul(u1, src_[:, 0:8], e1))
                op("vector", lambda e, dst=dst, t_a=t_a, e2=e2:
                   e.tensor_mul(dst[:, 0:8], t_a, e2))
                op("vector", lambda e, dst=dst, u1=u1:
                   e.tensor_add(dst[:, 0:8], dst[:, 0:8], u1))
                op("vector", lambda e, src_=src_, e1=e1, u1=u1:
                   e.tensor_mul(u1, src_[:, 8:16], e1))
                op("vector", lambda e, dst=dst, t_b=t_b, e2=e2:
                   e.tensor_mul(dst[:, 8:16], t_b, e2))
                carry_V = op("vector", lambda e, dst=dst, u1=u1:
                             e.tensor_add(dst[:, 8:16], dst[:, 8:16], u1), inc="V")
            exch["cfin"] = cA if (NGROUP - 1) % 2 == 0 else cB
            exch["carry_V"] = carry_V
    else:
        def emit_sq1():
            pass

        def emit_exchange():
            cfin = carr[:, 0:24]
            exch["cfin"] = cfin
            exch["carry_V"] = op("vector", lambda e: e.memset(cfin, 0.0),
                                 inc="V")

    fwb = alloc("fwb", [128, 2 * 4096], BF16)
    pool_b = alloc("pool_b", [128, 3 * PW], F32)
    S = S + [pool_b[:, i * PW:(i + 1) * PW] for i in range(3)]

    scan_state = {"done_V": None}
    ruse = {}

    def emit_scan_chunk(i):
        b, h = i // 2, i % 2
        w_b = prm[:, PC_W + b:PC_W + b + 1]
        w8_b = d_w8[:, b:b + 1]
        expw_b = d_expw[:, b:b + 1]
        expu_b = d_expu[:, b:b + 1]
        a_in = exch["cfin"][:, b:b + 1]
        b_in = exch["cfin"][:, 8 + b:9 + b]
        q_in = exch["cfin"][:, 16 + b:17 + b]
        kl = emit_kload(b, h)
        j = 2 * b + h
        if j < 2:
            wait("sync", "sp0", k_spills_done[0])
            wait("sync", "sp1", k_spills_done[1])
        else:
            wait("sync", "V", ruse[j - 2])
        rl = op("sync", lambda e, b=b, h=h, j=j:
                e.dma_start(out=RB[j % 2][:, :],
                            in_=r_hbm[b][:, h * HALF:(h + 1) * HALF]),
                inc="sqr" if j % 2 == 0 else "sqr1", n=16)
        vs = v_b(b)[:, h * HALF:(h + 1) * HALF]
        if b == 0 and h == 0:
            wait("vector", "V", exch["carry_V"])
            wait("vector", "T", vmm_T)
        wait("vector", "sqk", kl)
        op("vector", lambda e: e.tensor_reduce(
            C[0][:, 0:NSEGH], seg3(kstr[:, :]), axis=AX.X, op=ALU.max))
        if h == 0:
            op("vector", lambda e, q_in=q_in:
               e.tensor_copy(GB[:, 0:1], q_in))
        else:
            op("vector", lambda e: e.tensor_copy(GB[:, 0:1], hst[:, 2:3]))
        op("vector", lambda e, w8_b=w8_b: e.tensor_tensor_scan(
            GB[:, 1:1 + NSEGH], w8_b.broadcast_to([128, NSEGH]),
            C[0][:, 0:NSEGH], initial=GB[:, 0:1], op0=ALU.add, op1=ALU.max))
        vv = op("vector", lambda e: e.tensor_tensor(
            C[1][:, 0:NSEGH], GB[:, 0:NSEGH], GB[:, 1:1 + NSEGH],
            op=ALU.subtract), inc="V")
        wait("scalar", "V", vv)
        op("scalar", lambda e, w_b=w_b: e.activation(
            C[2][:, 0:NSEGH], C[1][:, 0:NSEGH], AF.Exp, bias=w_b))
        sv = op("scalar", lambda e: e.activation(
            C[3][:, 0:NSEGH], C[1][:, 0:NSEGH], AF.Exp, scale=-1.0),
            inc="S")
        kdone = op("vector", lambda e: e.tensor_tensor(
            seg3(S[1][:, 0:HALF]), seg3(kstr[:, :]),
            GB[:, 1:1 + NSEGH].unsqueeze(2).broadcast_to([128, NSEGH, SEG]),
            op=ALU.subtract), inc="V")
        kstr_guard_V[0] = kdone
        if h == 0:
            op("vector", lambda e, expw_b=expw_b: e.tensor_scalar(
                S[0][:, 0:HALF], S[1][:, 0:HALF], 0.0, expw_b,
                op0=ALU.mult, op1=ALU.add))
        wait("vector", "S", sv)
        vv = op("vector", lambda e: e.tensor_copy(
            seg3(S[0][:, 0:HALF])[:, :, 0:1],
            C[2][:, 0:NSEGH].unsqueeze(2)), inc="V")
        wait("scalar", "V", vv)
        sv = op("scalar", lambda e: e.activation(
            S[1][:, 0:HALF], S[1][:, 0:HALF], AF.Exp), inc="S")
        wait("vector", "S", sv)
        op("vector", lambda e, expu_b=expu_b: e.tensor_scalar_mul(
            S[2][:, 0:HALF], S[1][:, 0:HALF], expu_b))
        op("vector", lambda e: e.tensor_tensor(
            seg3(S[2][:, 0:HALF])[:, :, 0:1], seg3(S[2][:, 0:HALF])[:, :, 0:1],
            C[3][:, 0:NSEGH].unsqueeze(2), op=ALU.mult))
        op("vector", lambda e, vs=vs: e.tensor_tensor(
            S[3][:, 0:HALF], S[2][:, 0:HALF], vs, op=ALU.mult))
        op("vector", lambda e, vs=vs: e.tensor_tensor(
            S[4][:, 0:HALF], S[1][:, 0:HALF], vs, op=ALU.mult))
        if h == 0:
            op("vector", lambda e, a_in=a_in: e.tensor_copy(S[5][:, 0:1], a_in))
            op("vector", lambda e, b_in=b_in: e.tensor_copy(S[6][:, 0:1], b_in))
        else:
            op("vector", lambda e: e.tensor_copy(S[5][:, 0:1], hst[:, 0:1]))
            op("vector", lambda e: e.tensor_copy(S[6][:, 0:1], hst[:, 1:2]))
        op("vector", lambda e: e.tensor_tensor_scan(
            S[5][:, 1:1 + HALF], S[0][:, 0:HALF], S[4][:, 0:HALF],
            initial=S[5][:, 0:1], op0=ALU.mult, op1=ALU.add))
        op("vector", lambda e: e.tensor_tensor_scan(
            S[6][:, 1:1 + HALF], S[0][:, 0:HALF], S[1][:, 0:HALF],
            initial=S[6][:, 0:1], op0=ALU.mult, op1=ALU.add))
        op("vector", lambda e: e.tensor_copy(hst[:, 0:1],
                                             S[5][:, HALF:HALF + 1]))
        op("vector", lambda e: e.tensor_copy(hst[:, 1:2],
                                             S[6][:, HALF:HALF + 1]))
        op("vector", lambda e: e.tensor_copy(hst[:, 2:3],
                                             GB[:, NSEGH:NSEGH + 1]))
        op("vector", lambda e: e.tensor_tensor(
            S[4][:, 0:HALF], S[5][:, 0:HALF], S[3][:, 0:HALF], op=ALU.add))
        op("vector", lambda e: e.tensor_tensor(
            S[3][:, 0:HALF], S[6][:, 0:HALF], S[2][:, 0:HALF], op=ALU.add))
        op("vector", lambda e: e.reciprocal(
            S[1][:, 0:HALF], S[3][:, 0:HALF]))
        op("vector", lambda e: e.tensor_tensor(
            S[2][:, 0:HALF], S[4][:, 0:HALF], S[1][:, 0:HALF], op=ALU.mult))
        wait("vector", "sqr" if j % 2 == 0 else "sqr1", rl)
        rwh = rw16_b(b)[:, h * HALF:(h + 1) * HALF]
        scan_state["done_V"] = op("vector", lambda e, rwh=rwh, j=j:
                                  e.tensor_tensor(rwh, S[2][:, 0:HALF],
                                                  RB[j % 2][:, :],
                                                  op=ALU.mult), inc="V")
        ruse[j] = scan_state["done_V"]


    def gelu_spill_evac(sc_off):
        def evac(mp, pbank, mm_last):
            wait("scalar", "T", mm_last)
            sidx = spill_state["idx"] % 2
            spill_state["idx"] += 1
            prev_dma = spill_state["last_dma"][sidx]
            if prev_dma:
                wait("scalar", "sp%d" % sidx, prev_dma)
            if mp == 0:
                wait("scalar", "V", a_done_V)
            e_last = None
            for mi in range(2):
                m = mp * 2 + mi
                sc = prm[:, sc_off + m:sc_off + m + 1]
                e_last = op("scalar",
                            lambda e, mi=mi, sc=sc, pbank=pbank, sidx=sidx:
                            e.activation(FPs16[sidx][:, mi * SW:(mi + 1) * SW],
                                         pbank[:, mi * PSB:mi * PSB + SW],
                                         AF.Gelu, scale=sc), inc="S")
            wait("sync", "S", e_last)
            dmac = op("sync", lambda e, mp=mp, sidx=sidx:
                      e.dma_start(out=b_hbm[mp], in_=FPs16[sidx][:, 0:2 * SW]),
                      inc="sp%d" % sidx, n=16)
            spill_state["last_dma"][sidx] = dmac
            return "S", e_last
        return evac

    fwb_guard = [None]
    SLOT = {mp: [mp - 4] for mp in range(4, NHB // 2)}

    def scan_slots(mp):
        if mp <= 2:
            emit_p1_block(NB - P1_LAG + mp)
        if mp == 3:
            emit_sq1()
        for j in SLOT.get(mp, []):
            emit_scan_chunk(j)

    h1_done = emit_h(wsh1, lambda kb: abuf[:, kb * SW:(kb + 1) * SW],
                     gelu_spill_evac(PC_SH1),
                     [("V", a_done_V), ("S", vev_S)], fwb_guard,
                     xn_guard=True, interleave=scan_slots,
                     preload=h1_pre, inject={4: emit_exchange},
                     pre0_rbb=h1_rbb)
    for j in range(NHB // 2 - 4, 2 * NB):
        emit_scan_chunk(j)
    b_spills_done = dict(spill_state["last_dma"])

    scan_done_V = scan_state["done_V"]

    rw_V = scan_state["done_V"]
    b16 = v32[:, :].bitcast(BF16)
    bload = {}
    for mp in range(1, NHB // 2, 2):
        if mp == 1:
            wait("gpsimd", "V", scan_state["done_V"])
            wait("gpsimd", "sp0", b_spills_done[0])
            wait("gpsimd", "sp1", b_spills_done[1])
        bload[mp] = op("gpsimd", lambda e, mp=mp:
                       e.dma_start(out=b16[:, mp * 2 * SW:(mp + 1) * 2 * SW],
                                   in_=b_hbm[mp]), inc="sqb1", n=16)
    omm_T, oev_S = emit_proj(wo_d, rw16_b, rw_V,
                             [h1_done[NHB // 2 - 2]],
                             spill_evac_f32(acc_hbm), wbuf_guard=w12mm_T,
                             psum_guards1=[h1_done[NHB // 2 - 1]])
    acc_sp_done = dict(spill_state["last_dma"])

    free("rbb", "hst", "cpa", "kstr", "pool_a")
    finpool = alloc("finpool", [128, 4 * 1024], F32, side="right")
    FN = [finpool[:, i * 1024:(i + 1) * 1024] for i in range(4)]

    for mp in range(0, NHB // 2, 2):
        if mp == 0:
            wait("sync", "T", omm_T)
            wait("sync", "sp0", b_spills_done[0])
            wait("sync", "sp1", b_spills_done[1])
        bload[mp] = op("sync", lambda e, mp=mp:
                       e.dma_start(out=b16[:, mp * 2 * SW:(mp + 1) * 2 * SW],
                                   in_=b_hbm[mp]), inc="sqb", n=16)

    def gelu_evac(dst16, sc_off):
        def evac(mp, pbank, mm_last):
            wait("scalar", "T", mm_last)
            e_last = None
            for mi in range(2):
                m = mp * 2 + mi
                sc = prm[:, sc_off + m:sc_off + m + 1]
                dap = dst16(m)
                e_last = op("scalar",
                            lambda e, mi=mi, sc=sc, pbank=pbank, dap=dap:
                            e.activation(dap, pbank[:, mi * PSB:mi * PSB + SW],
                                         AF.Gelu, scale=sc), inc="S")
            return "S", e_last
        return evac

    if debug_dump:
        wait("sync", "sqb", bload[NHB // 2 - 1])
        op("sync", lambda e: e.dma_start(out=dbg_b, in_=b16[:, :]),
           inc="sq", n=16)
        wait("sync", "sq", cnt["sq"])
    wait("tensor", "sqb", bload[NHB // 2 - 2])
    wait("tensor", "sqb1", bload[NHB // 2 - 1])
    h2_done = emit_h(wsh2, lambda kb: b16[:, kb * SW:(kb + 1) * SW],
                     gelu_evac(lambda m: abuf[:, m * SW:(m + 1) * SW], PC_SH2),
                     [("S", oev_S)], fwb_guard)

    tful = v32

    def emit_w3_fin(st, c_src, c_done):
        w3mm_T = {}
        w3mm_half = {}
        w3ev_V = {}
        acc_rl = {}
        for mp in range(NB // 2):
            wds = {}
            for hf in range(2):
                if mp == 0 and fwb_guard[0]:
                    wait(WD_ENG[hf], "T", fwb_guard[0])
                if mp >= 1:
                    wait(WD_ENG[hf], "T", w3mm_half[(mp - 1, hf)])
                wds[hf] = emit_h_wd(ws3, mp, hf)
            for mi in range(2):
                m = mp * 2 + mi
                if mp >= 1:
                    wait("sync", "V", w3ev_V[mp - 1])
                if st == 0 and mp == 0 and mi == 0:
                    wait("sync", "sp0", acc_sp_done[0])
                    wait("sync", "sp1", acc_sp_done[1])
                if st > 0 and mp == 0 and mi == 0:
                    wait("sync", "out0", cnt["out0"])
                    wait("sync", "out1", cnt["out1"])
                acc_rl[m] = op("sync", lambda e, m=m, mi=mi, st=st:
                               e.dma_start(out=FN[2 + mi][:, 0:SW],
                                           in_=acc_hbm[m][:, st * SW:
                                                          (st + 1) * SW]),
                               inc="sqa%d" % mi, n=16)
            pbank = pA if (mp % 2 == 0) else pB
            if mp == 0:
                sem, val = c_done
                wait("tensor", sem, val)
            if mp >= 2:
                wait("tensor", "V", w3ev_V[mp - 2])
            last = None
            for hf in range(2):
                for kb in range(16):
                    if kb % 8 == 0:
                        wait("tensor", "gq%d_%d" % (hf, kb // 8),
                             wds[hf][kb // 8])
                    kbg = hf * 16 + kb
                    for mi in range(2):
                        for n in range(0, SW, NW):
                            nwid = min(NW, SW - n)
                            mov = c_src(kbg)[:, n:n + nwid]
                            last = op(
                                "tensor",
                                lambda e, mi=mi, n=n, nwid=nwid, pbank=pbank,
                                mov=mov, kbg=kbg:
                                e.matmul(
                                    pbank[:, mi * PSB + n:mi * PSB + n + nwid],
                                    fwb[:, (kbg // 16) * 4096 + (kbg % 16) * 256
                                         + mi * 128:
                                         (kbg // 16) * 4096 + (kbg % 16) * 256
                                         + mi * 128 + 128],
                                    mov, start=(kbg == 0), stop=(kbg == 31)),
                                inc="T")
                w3mm_half[(mp, hf)] = last
            w3mm_T[mp] = last
            e_last = None
            wait("vector", "T", last)
            for mi in range(2):
                m = mp * 2 + mi
                s3_m = prm[:, PC_S3 + m:PC_S3 + m + 1]
                wait("vector", "sqa%d" % mi, acc_rl[m])
                e_last = op("vector",
                            lambda e, mi=mi, m=m, s3_m=s3_m, pbank=pbank:
                            e.scalar_tensor_tensor(
                                tful[:, m * SW:(m + 1) * SW],
                                pbank[:, mi * PSB:mi * PSB + SW], s3_m,
                                FN[2 + mi][:, 0:SW], op0=ALU.mult, op1=ALU.add),
                            inc="V")
            w3ev_V[mp] = e_last
        t_done_V = w3ev_V[NB // 2 - 1]
        fwb_guard[0] = w3mm_T[NB // 2 - 1]

        n_tc = SW // 128
        fin_V = {}
        out_cnt_l = {}
        for tc in range(n_tc):
            sl = tc % 2
            if tc == 0:
                wait("sync", "V", t_done_V)
            if tc >= 2:
                wait("sync", "V", fin_V[tc - 2])
            x_rl = op("sync", lambda e, tc=tc, st=st, sl=sl:
                      e.dma_start(out=FN[sl][:, 0:D],
                                  in_=xs[1 + st * SW + tc * 128:
                                         1 + st * SW + tc * 128 + 128, :]),
                      inc="sqf%d" % sl, n=16)
            pbank = pA if (tc % 2 == 0) else pB
            if tc == 0:
                wait("tensor", "V", t_done_V)
            if tc >= 2:
                wait("tensor", "V", fin_V[tc - 2])
            tl = None
            for m in range(NB):
                tl = op("tensor", lambda e, tc=tc, m=m, pbank=pbank:
                        e.transpose(pbank[:, m * 128:(m + 1) * 128],
                                    tful[:, m * SW + tc * 128:
                                         m * SW + tc * 128 + 128],
                                    idt[:, :]), inc="T")
            wait("vector", "T", tl)
            wait("vector", "sqf%d" % sl, x_rl)
            if tc >= 2:
                wait("vector", "out%d" % sl, out_cnt_l[tc - 2])
            fin_V[tc] = op("vector", lambda e, tc=tc, pbank=pbank, sl=sl:
                           e.tensor_tensor(FN[2 + sl][:, 0:D], pbank[:, 0:D],
                                           FN[sl][:, 0:D], op=ALU.add), inc="V")
            wait("sync", "V", fin_V[tc])
            out_cnt_l[tc] = op("sync", lambda e, tc=tc, st=st, sl=sl:
                               e.dma_start(out=y[st * SW + tc * 128:
                                                 st * SW + tc * 128 + 128, :],
                                           in_=FN[2 + sl][:, 0:D]),
                               inc="out%d" % sl, n=16)
        return fin_V[n_tc - 1], w3mm_T[NB // 2 - 1]

    if debug_dump:
        wait("sync", "S", h2_done[NHB // 2 - 1][1])
        op("sync", lambda e: e.dma_start(out=dbg_c, in_=abuf[:, :]),
           inc="sq", n=16)
        wait("sync", "sq", cnt["sq"])
    fin0_V, w3mm0_T = emit_w3_fin(0, lambda kb: abuf[:, kb * SW:(kb + 1) * SW],
                                  h2_done[NHB // 2 - 1])

    free("pool_b")
    xnT2 = alloc("xnT2", [128, NB * (T + 1)], BF16, side="right")
    wait("sync", "V", scan_done_V)
    wait("sync", "T", omm_T)
    xrl = op("sync", lambda e: e.dma_start(out=xnT2[:, :], in_=xn_hbm),
             inc="sqxr", n=16)
    wait("tensor", "sqxr", xrl)

    w12mm1_T, a1_done_V = emit_w12(
        1, lambda kb: xnT_b(kb, xnT2), [("V", fin0_V)],
        omm_T, w3mm0_T, lambda m: abuf[:, m * SW:(m + 1) * SW],
        lambda s: fwb[:, s * 4096:s * 4096 + 1024])

    fwb_guard[0] = w12mm1_T
    h1s1_done = emit_h(wsh1, lambda kb: abuf[:, kb * SW:(kb + 1) * SW],
                       gelu_evac(lambda m: b16[:, m * SW:(m + 1) * SW], PC_SH1),
                       [("V", a1_done_V)], fwb_guard)
    h2s1_done = emit_h(wsh2, lambda kb: b16[:, kb * SW:(kb + 1) * SW],
                       gelu_evac(lambda m: abuf[:, m * SW:(m + 1) * SW], PC_SH2),
                       [h1s1_done[NHB // 2 - 1]], fwb_guard)
    fin1_V, _ = emit_w3_fin(1, lambda kb: abuf[:, kb * SW:(kb + 1) * SW],
                            h2s1_done[NHB // 2 - 1])

    wait("sync", "out0", cnt["out0"])
    wait("sync", "out1", cnt["out1"])

    free("xnT2")
    free("finpool")
    free("fwb")

    from contextlib import ExitStack
    with ExitStack() as _ctx:
        for name in SEMNAMES:
            sems[name] = _ctx.enter_context(nc.semaphore("s_" + name))
        block = _ctx.enter_context(nc.Block())

        @block.sync
        def _(eng):
            for f in streams["sync"]:
                f(eng)

        @block.gpsimd
        def _(eng):
            for f in streams["gpsimd"]:
                f(eng)

        @block.vector
        def _(eng):
            for f in streams["vector"]:
                f(eng)

        @block.scalar
        def _(eng):
            for f in streams["scalar"]:
                f(eng)

        @block.tensor
        def _(eng):
            for f in streams["tensor"]:
                f(eng)

    psg[1].__exit__(None, None, None)
    psg[0].__exit__(None, None, None)
    for g in reversed(list(guards)):
        guards.pop(g).__exit__(None, None, None)
    return nc


def _host_prep(inputs, T=2048, ncore=NCORE):
    x = np.ascontiguousarray(np.asarray(inputs["L0_attn_x"], np.float32))
    B = x.shape[0]

    def colpack(v, nb):
        return np.ascontiguousarray(np.asarray(v, np.float32).reshape(nb, 128).T)

    pcol = np.zeros((128, PC_TOT), np.float32)
    pcol[:, PC_W:PC_W + 8] = colpack(inputs["L0_attn_time_decay"], 8)
    pcol[:, PC_U:PC_U + 8] = colpack(inputs["L0_attn_time_first"], 8)
    pcol[:, PC_TMK:PC_TMK + 8] = colpack(np.asarray(inputs["L0_attn_time_mix_k"]).ravel(), 8)
    pcol[:, PC_TMV:PC_TMV + 8] = colpack(np.asarray(inputs["L0_attn_time_mix_v"]).ravel(), 8)
    pcol[:, PC_TMR:PC_TMR + 8] = colpack(np.asarray(inputs["L0_attn_time_mix_r"]).ravel(), 8)
    pcol[:, PC_S3:PC_S3 + 8] = colpack(inputs["L0_ffn_w3_s"], 8)
    pcol[:, PC_S1:PC_S1 + 32] = colpack(inputs["L0_ffn_w1_s"], 32)
    pcol[:, PC_S2:PC_S2 + 32] = colpack(inputs["L0_ffn_w2_s"], 32)
    pcol[:, PC_SH1:PC_SH1 + 32] = colpack(inputs["L0_ffn_h1_s"], 32)
    pcol[:, PC_SH2:PC_SH2 + 32] = colpack(inputs["L0_ffn_h2_s"], 32)

    def tr(a):
        return np.ascontiguousarray(np.asarray(a, np.float32).T)

    weights = {
        "wkT": tr(inputs["L0_attn_W_k"]), "wvT": tr(inputs["L0_attn_W_v"]),
        "wrT": tr(inputs["L0_attn_W_r"]), "woT": tr(inputs["L0_attn_W_o"]),
        "w1T": tr(inputs["L0_ffn_w1_w"]), "w2T": tr(inputs["L0_ffn_w2_w"]),
        "h1T": tr(inputs["L0_ffn_h1_w"]), "h2T": tr(inputs["L0_ffn_h2_w"]),
        "w3T": tr(inputs["L0_ffn_w3_w"]),
        "ident": np.eye(128, dtype=np.float32),
        "pcol": pcol,
    }
    chunks_per_seq = ncore // B
    in_maps = []
    for c in range(ncore):
        bidx = c // chunks_per_seq
        j = c % chunks_per_seq
        t0 = j * T
        shard = np.empty((T + 1, D), np.float32)
        shard[1:] = x[bidx, t0:t0 + T]
        shard[0] = 0.0 if j == 0 else x[bidx, t0 - 1]
        cm = np.zeros((128, NGROUP), np.float32)
        cm[:, :j] = 1.0
        in_maps.append(dict(weights, xs=shard, cmask=cm))
    return in_maps


def kernel(**inputs):
    T = 2048
    in_maps = _host_prep(inputs, T)
    nc = build_nc(T=T, use_collective=True)
    res = run_bass_kernel_spmd(nc, in_maps, core_ids=list(range(NCORE)))
    x = inputs["L0_attn_x"]
    B, TT, _ = x.shape
    out = np.empty((B, TT, D), np.float32)
    chunks_per_seq = NCORE // B
    for c in range(NCORE):
        bidx = c // chunks_per_seq
        j = c % chunks_per_seq
        out[bidx, j * T:(j + 1) * T] = res.results[c]["y"]
    return out
